# version 06
# baseline (speedup 1.0000x reference)
import numpy as np
import ml_dtypes
import concourse.bass as bass
import concourse.mybir as mybir
from concourse.bass_utils import run_bass_kernel_spmd
from contextlib import ExitStack

F32 = mybir.dt.float32
BF16 = mybir.dt.bfloat16
I32 = mybir.dt.int32
AF = mybir.ActivationFunctionType
ALU = mybir.AluOpType

CFG = dict(L=8192, T=512, NLAYER=4)
D = 1024
DIN = 2048
S = 16
FF = 4096
NDB = DIN // 128
NDO = D // 128
NFB = FF // 128
PITCH = 514
EPS = 1e-5


class _SinkInst:
    def then_inc(self, *a, **k):
        return self


_SINKINST = _SinkInst()


class _Sink:
    def __getattr__(self, k):
        return lambda *a, **kw: _SINKINST


_SINK = _Sink()


class MK:
    def __init__(self, nc, cfg):
        self.nc = nc
        self.cfg = cfg
        self.L = cfg["L"]
        self.T = cfg["T"]
        self.NCH = self.L // self.T
        self.NL = cfg["NLAYER"]

    def E(self, name):
        return self.eng if self.engine == name else _SINK

    @property
    def v(self):
        return self.E("vector")

    @property
    def a(self):
        return self.E("scalar")

    @property
    def t(self):
        return self.E("tensor")

    @property
    def g(self):
        return self.E("gpsimd")

    @property
    def q(self):
        return self.E("sync")

    def inc(self, name, inst, sem, n=1):
        if self.engine == name:
            inst.then_inc(self.sems[sem], n)
        self.c[sem] += n
        return self.c[sem]

    def wait(self, name, sem, val=None):
        if self.engine == name:
            self.eng.wait_ge(self.sems[sem], self.c[sem] if val is None else val)

    def vi(self, inst, n=1):
        return self.inc("vector", inst, "VD", n)

    def ai(self, inst, n=1):
        return self.inc("scalar", inst, "AD", n)

    def ti(self, inst, n=1):
        return self.inc("tensor", inst, "TD", n)

    def qi(self, inst):
        m = self.inc("sync", inst, "LD", 16)
        self.wait("sync", "LD", m)
        return m

    def gi(self, inst):
        m = self.inc("gpsimd", inst, "GD", 16)
        self.wait("gpsimd", "GD", m)
        return m

    def pi(self, inst, n=1):
        return self.inc("gpsimd", inst, "PD", n)

    def vc(self, inst):
        m = self.vi(inst)
        self.wait("vector", "VD", m)
        return m

    def ac(self, inst):
        m = self.ai(inst)
        self.wait("scalar", "AD", m)
        return m

    def run(self, engine, eng):
        self.engine, self.eng = engine, eng
        self.c = {k: 0 for k in self.sems}
        self.script()

    def barrier(self):
        for e in ("sync", "vector", "scalar", "tensor", "gpsimd"):
            for s in self.sems:
                self.wait(e, s)

    def script(self):
        if self.cfg.get("stage"):
            self.prologue()
            self.barrier()
            self.ffn_phase(p3=False)
            self.barrier()
            return
        self.prologue()
        self.barrier()
        self.ffn_phase(p3=False)
        for dl in range(2 * self.NL):
            self.mamba_layer(dl)
        self.ffn_phase(p3=True)
        self.barrier()
        if self.cfg.get("dump"):
            for nm in ("a", "xh", "fs", "bs", "rsA", "rsB"):
                self.gi(self.g.dma_start(out=self.i["d_" + nm],
                                         in_=self.s[nm]))
            self.barrier()

    def build(self):
        nc = self.nc
        L, T = self.L, self.T
        NDL = 2 * self.NL

        def dram(name, shape, dt, kind="ExternalInput"):
            return nc.dram_tensor(name, shape, dt, kind=kind).ap()

        i = {}
        i["xT"] = dram("xT", [L // T, 128, NDO * T], F32)
        i["inwu"] = dram("inwu", [NDL, 128, NDB * 8 * 128], F32)
        i["inwz"] = dram("inwz", [NDL, 128, NDB * 8 * 128], F32)
        i["outw"] = dram("outw", [NDL, 128, NDO * NDB * 128], F32)
        i["xpwa"] = dram("xpwa", [NDL, 128, 16 * 80], F32)
        i["xpwb"] = dram("xpwb", [NDL, 128, 16 * 16], F32)
        i["dtw"] = dram("dtw", [NDL, 64, DIN], F32)
        i["alog"] = dram("alog", [NDL, 128, NDB * S], F32)
        i["cw"] = dram("cw", [NDL, 128, NDB * 4], F32)
        i["cb"] = dram("cb", [NDL, 128, NDB], F32)
        i["dtb"] = dram("dtb", [NDL, 128, NDB], F32)
        i["Dp"] = dram("Dp", [NDL, 128, NDB], F32)
        i["nw"] = dram("nw", [NDL, 128, NDO], F32)
        i["nb"] = dram("nb", [NDL, 128, NDO], F32)
        i["fc1q"] = dram("fc1q", [128, NFB * 8 * 128], I32)
        i["fc2q"] = dram("fc2q", [128, NDO * NFB * 128], I32)
        i["fc1s"] = dram("fc1s", [128, NFB * 8 * 128], F32)
        i["fc2s"] = dram("fc2s", [128, NDO * NFB * 128], F32)
        i["b1"] = dram("b1", [128, NFB], F32)
        i["b2"] = dram("b2", [128, NDO], F32)
        i["lng"] = dram("lng", [128, NDO], F32)
        i["lnb"] = dram("lnb", [128, NDO], F32)
        i["nfw"] = dram("nfw", [128, NDO], F32)
        i["nfb"] = dram("nfb", [128, NDO], F32)
        i["mgw"] = dram("mgw", [128, NDO * 16 * 128], F32)
        i["mgb"] = dram("mgb", [128, NDO], F32)
        i["onesw"] = dram("onesw", [128, 2], F32)
        i["onescol"] = dram("onescol", [1, 128], F32)
        i["ident"] = dram("ident", [128, 128], BF16)
        i["kons"] = dram("kons", [128, 2], F32)
        i["out"] = dram("out", [L // T, 128, NDO * T], F32, kind="ExternalOutput")
        if self.cfg.get("dump"):
            for nm in ("a", "xh", "fs", "bs", "rsA", "rsB"):
                i["d_" + nm] = dram("d_" + nm, [L // T, 128, NDO * T], F32,
                                    kind="ExternalOutput")
        self.i = i

        s = {}
        s["wc_inwu"] = dram("wc_inwu", [NDL, 128, NDB * 8 * 128], BF16, "Internal")
        s["wc_inwz"] = dram("wc_inwz", [NDL, 128, NDB * 8 * 128], BF16, "Internal")
        s["wc_outw"] = dram("wc_outw", [NDL, 128, NDO * NDB * 128], BF16, "Internal")
        s["wc_w1"] = dram("wc_w1", [128, NFB * 8 * 128], BF16, "Internal")
        s["wc_w2"] = dram("wc_w2", [128, NDO * NFB * 128], BF16, "Internal")
        s["wc_mg"] = dram("wc_mg", [128, NDO * 16 * 128], BF16, "Internal")
        s["xh"] = dram("sc_xh", [L // T, 128, NDO * T], F32, "Internal")
        s["a"] = dram("sc_a", [L // T, 128, NDO * T], F32, "Internal")
        s["rsA"] = dram("sc_rsA", [L // T, 128, NDO * T], F32, "Internal")
        s["rsB"] = dram("sc_rsB", [L // T, 128, NDO * T], F32, "Internal")
        s["fs"] = dram("sc_fs", [L // T, 128, NDO * T], F32, "Internal")
        s["bs"] = dram("sc_bs", [L // T, 128, NDO * T], F32, "Internal")
        s["bflat"] = dram("sc_bflat", [1, S * PITCH + 2], BF16, "Internal")
        s["cflat"] = dram("sc_cflat", [1, S * PITCH + 2], BF16, "Internal")
        self.s = s

        with ExitStack() as ctx:
            def sb(name, shape, dt):
                return ctx.enter_context(nc.sbuf_tensor(name, shape, dt))

            b = {}
            b["rc"] = sb("rc", [128, NDO * T], F32)
            b["hn"] = sb("hn", [128, NDO * T], BF16)
            b["u0"] = sb("u0", [128, NDB * 516], BF16)
            b["zy"] = sb("zy", [128, NDB * T], BF16)
            b["dA0"] = sb("dA0", [128, S * PITCH], BF16)
            b["dA1"] = sb("dA1", [128, S * PITCH], BF16)
            b["qq"] = sb("qq", [128, S * PITCH], BF16)
            b["h"] = sb("h", [128, S * PITCH], BF16)
            b["bsl"] = sb("bsl", [128, S * PITCH + 2], BF16)
            b["csl"] = sb("csl", [128, S * PITCH + 2], BF16)
            b["wru"] = sb("wru", [128, 2 * 2048], BF16)
            b["wrz"] = sb("wrz", [128, 2 * 2048], BF16)
            b["wro"] = sb("wro", [128, 4096], BF16)
            b["wrf"] = sb("wrf", [128, 1024], F32)
            b["wrb"] = sb("wrb", [128, 2048], BF16)
            b["dtws"] = sb("dtws", [64, DIN], BF16)
            b["dtr"] = sb("dtr", [64, T], BF16)
            b["brow"] = sb("brow", [16, T], BF16)
            b["crow"] = sb("crow", [16, T], BF16)
            b["dt"] = sb("dt", [128, 2 * T], BF16)
            b["dtu"] = sb("dtu", [128, 516], BF16)
            b["diag"] = sb("diag", [128, 4 * 128], BF16)
            b["xpa"] = sb("xpa", [128, 16 * 80], BF16)
            b["xpb"] = sb("xpb", [128, 16 * 16], BF16)
            b["st"] = sb("st", [128, NDB * S], BF16)
            b["A"] = sb("A", [128, NDB * S], F32)
            b["cws"] = sb("cws", [128, NDB * 4], F32)
            b["pp"] = sb("pp", [128, 16], F32)
            b["pq"] = sb("pq", [128, NDB * 3], F32)
            b["pf"] = sb("pf", [128, NFB + 3 * NDO], F32)
            b["pg"] = sb("pg", [128, 3 * NDO], F32)
            b["onesw"] = sb("onesws", [128, 2], F32)
            b["onescol"] = sb("onescols", [1, 128], F32)
            b["ident"] = sb("idents", [128, 128], BF16)
            b["kons"] = sb("konss", [128, 2], F32)
            b["t2"] = sb("t2", [128, T], F32)
            b["ut"] = sb("ut", [128, NDB * 4], BF16)
            self.b = b

            p = {}
            for k in range(6):
                p[f"pb{k}"] = ctx.enter_context(
                    nc.psum_tensor(f"pb{k}", [128, T], F32))
            p["ps1"] = ctx.enter_context(nc.psum_tensor("ps1", [1, T], F32))
            p["ps2"] = ctx.enter_context(nc.psum_tensor("ps2", [1, T], F32))
            self.p = p

            sems = {}
            for name in ("LD", "GD", "TD", "AD", "VD", "PD"):
                sems[name] = ctx.enter_context(nc.semaphore(name))
            self.sems = sems

            block = ctx.enter_context(nc.Block())

            @block.sync
            def _(e):
                self.run("sync", e)

            @block.vector
            def _(e):
                self.run("vector", e)

            @block.scalar
            def _(e):
                self.run("scalar", e)

            @block.tensor
            def _(e):
                self.run("tensor", e)

            @block.gpsimd
            def _(e):
                self.run("gpsimd", e)

        return nc

    def prologue(self):
        b, i, s = self.b, self.i, self.s
        NDL = 2 * self.NL
        self._pro = {"i": 0, "cast": [], "store": []}
        self.qi(self.q.dma_start(out=b["onesw"][:], in_=i["onesw"]))
        self.qi(self.q.dma_start(out=b["onescol"][:], in_=i["onescol"]))
        self.qi(self.q.dma_start(out=b["ident"][:], in_=i["ident"]))
        self.qi(self.q.dma_start(out=b["kons"][:], in_=i["kons"]))
        self.qi(self.q.dma_start(out=b["pf"][:, 0:NFB], in_=i["b1"]))
        self.qi(self.q.dma_start(out=b["pf"][:, NFB:NFB + NDO], in_=i["b2"]))
        self.qi(self.q.dma_start(out=b["pf"][:, NFB + NDO:NFB + 2 * NDO],
                                 in_=i["lng"]))
        self.qi(self.q.dma_start(out=b["pf"][:, NFB + 2 * NDO:NFB + 3 * NDO],
                                 in_=i["lnb"]))
        self.qi(self.q.dma_start(out=b["pg"][:, 0:NDO], in_=i["nfw"]))
        self.qi(self.q.dma_start(out=b["pg"][:, NDO:2 * NDO], in_=i["nfb"]))
        self.qi(self.q.dma_start(out=b["pg"][:, 2 * NDO:3 * NDO], in_=i["mgb"]))
        self.pi(self.g.memset(b["dtu"][:], 0.0))
        self.pi(self.g.memset(b["qq"][:], 0.0))
        self.pi(self.g.memset(b["h"][:], 0.0))
        m0 = self.pi(self.g.memset(b["bsl"][:], 0.0))
        self.wait("gpsimd", "PD", m0)
        self.gi(self.g.dma_start(out=s["bflat"],
                                 in_=b["bsl"][0:1, 0:S * PITCH + 2]))
        self.gi(self.g.dma_start(out=s["cflat"],
                                 in_=b["bsl"][0:1, 0:S * PITCH + 2]))

        for nm_src, nm_dst in (("inwu", "wc_inwu"), ("inwz", "wc_inwz")):
            for dl in range(NDL):
                for c0 in range(0, NDB * 1024, 1024):
                    self.cast_slab(i[nm_src][dl][:, c0:c0 + 1024],
                                   s[nm_dst][dl][:, c0:c0 + 1024], 1024)
        for dl in range(NDL):
            for c0 in range(0, NDO * 2048, 1024):
                self.cast_slab(i["outw"][dl][:, c0:c0 + 1024],
                               s["wc_outw"][dl][:, c0:c0 + 1024], 1024)
        for c0 in range(0, NDO * 2048, 1024):
            self.cast_slab(i["mgw"][:, c0:c0 + 1024],
                           s["wc_mg"][:, c0:c0 + 1024], 1024)
        for c0 in range(0, NFB * 1024, 1024):
            self.deq_slab(i["fc1q"][:, c0:c0 + 1024],
                          i["fc1s"][:, c0:c0 + 1024],
                          s["wc_w1"][:, c0:c0 + 1024])
        for c0 in range(0, NDO * NFB * 128, 1024):
            self.deq_slab(i["fc2q"][:, c0:c0 + 1024],
                          i["fc2s"][:, c0:c0 + 1024],
                          s["wc_w2"][:, c0:c0 + 1024])

    def cast_slab(self, src_ap, dst_ap, n):
        b = self.b
        wf3 = b["wrf"][:, 0:n].rearrange("p (k n) -> p k n", n=128)
        wb3 = b["wrb"][:, 0:n].rearrange("p (k n) -> p k n", n=128)
        self.wait("sync", "VD")
        m = self.qi(self.q.dma_start(out=wf3, in_=src_ap))
        self.wait("vector", "LD", m)
        self.wait("vector", "GD")
        mv = self.vi(self.v.tensor_copy(out=b["wrb"][:, 0:n],
                                        in_=b["wrf"][:, 0:n]))
        self.wait("gpsimd", "VD", mv)
        self.gi(self.g.dma_start(out=dst_ap, in_=wb3))

    def deq_slab(self, q_ap, s_ap, dst_ap):
        b = self.b
        n = 1024
        self.wait("sync", "VD")
        m = self.qi(self.q.dma_start(out=b["wrf"][:, 0:n].bitcast(I32),
                                     in_=q_ap))
        m2 = self.qi(self.q.dma_start(out=b["rc"][:, 0:n], in_=s_ap))
        self.wait("vector", "LD", m2)
        self.wait("vector", "GD")
        self.vc(self.v.tensor_copy(out=b["wrf"][:, 0:n],
                                   in_=b["wrf"][:, 0:n].bitcast(I32)))
        mv = self.vi(self.v.scalar_tensor_tensor(
            out=b["wrb"][:, 0:n], in0=b["wrf"][:, 0:n], scalar=-128.0,
            in1=b["rc"][:, 0:n], op0=ALU.add, op1=ALU.mult))
        self.wait("gpsimd", "VD", mv)
        self.gi(self.g.dma_start(
            out=dst_ap,
            in_=b["wrb"][:, 0:n].rearrange("p (k n) -> p k n", n=128)))

    def layernorm(self, src_f32_ap, gamma=None, beta=None, out_bf16=None,
                  out_f32=None, nsec=8, pbA="pb0", pbB="pb1"):
        b, p = self.b, self.p
        T = self.T
        sq = b["zy"][:].bitcast(F32)
        ma = self.ai(self.a.activation(out=sq[:, 0:nsec * T], in_=src_f32_ap,
                                       func=AF.Square))
        for k in range(nsec):
            mt1 = self.ti(self.t.matmul(p["ps1"][:], b["onesw"][:, 0:1],
                                        src_f32_ap[:, k * T:(k + 1) * T],
                                        start=(k == 0), stop=(k == nsec - 1)))
        self.wait("tensor", "AD", ma)
        for k in range(nsec):
            mt2 = self.ti(self.t.matmul(p["ps2"][:], b["onesw"][:, 0:1],
                                        sq[:, k * T:(k + 1) * T],
                                        start=(k == 0), stop=(k == nsec - 1)))
        rowsf = b["u0"][:].bitcast(F32)
        r_a = rowsf[0:1, 0:T]
        r_b = rowsf[0:1, T:2 * T]
        r_t = rowsf[0:1, 2 * T:3 * T]
        self.wait("scalar", "TD", mt1)
        ma = self.ac(self.a.activation(out=r_t, in_=p["ps1"][:], func=AF.Square))
        self.wait("vector", "TD", mt2)
        self.wait("vector", "AD", ma)
        mv = self.vc(self.v.tensor_tensor(out=r_t, in0=p["ps2"][:], in1=r_t,
                                          op=ALU.subtract))
        self.wait("scalar", "VD", mv)
        self.ac(self.a.activation(out=r_t, in_=r_t, func=AF.Ln,
                                  bias=b["kons"][0:1, 0:1]))
        ma = self.ac(self.a.activation(out=r_a, in_=r_t, func=AF.Exp,
                                       scale=-0.5))
        self.wait("vector", "AD", ma)
        mv = self.vc(self.v.tensor_tensor(out=r_b, in0=p["ps1"][:], in1=r_a,
                                          op=ALU.mult))
        self.wait("tensor", "VD", mv)
        self.ti(self.t.matmul(p[pbA][:], b["onescol"][:], r_a,
                              start=True, stop=True))
        mtb = self.ti(self.t.matmul(p[pbB][:], b["onescol"][:], r_b,
                                    start=True, stop=True))
        self.wait("vector", "TD", mtb)
        mv = 0
        for k in range(nsec):
            self.vc(self.v.tensor_tensor(out=b["t2"][:],
                                         in0=src_f32_ap[:, k * T:(k + 1) * T],
                                         in1=p[pbA][:], op=ALU.mult))
            mv = self.vc(self.v.tensor_tensor(out=b["t2"][:], in0=b["t2"][:],
                                              in1=p[pbB][:], op=ALU.subtract))
            if out_f32 is not None:
                mv = self.vc(self.v.tensor_copy(
                    out=out_f32[:, k * T:(k + 1) * T], in_=b["t2"][:]))
            if out_bf16 is not None:
                if gamma is not None:
                    mv = self.vc(self.v.tensor_scalar(
                        out=out_bf16[:, k * T:(k + 1) * T], in0=b["t2"][:],
                        scalar1=gamma[k], scalar2=beta[k],
                        op0=ALU.mult, op1=ALU.add))
                else:
                    mv = self.vc(self.v.tensor_copy(
                        out=out_bf16[:, k * T:(k + 1) * T], in_=b["t2"][:]))
        return mv

    def mamba_layer(self, dl):
        b, i, s = self.b, self.i, self.s
        fw = dl < self.NL
        li = dl % self.NL
        names = ["xh"] + ["rsA", "rsB"] * 4
        src = names[li]
        dst = ("fs" if fw else "bs") if li == self.NL - 1 else names[li + 1]

        m_alog = self.qi(self.q.dma_start(out=b["A"][:], in_=i["alog"][dl]))
        self.qi(self.q.dma_start(out=b["cws"][:], in_=i["cw"][dl]))
        self.qi(self.q.dma_start(out=b["pq"][:, 0:NDB], in_=i["dtb"][dl]))
        self.qi(self.q.dma_start(out=b["pq"][:, NDB:2 * NDB], in_=i["cb"][dl]))
        self.qi(self.q.dma_start(out=b["pq"][:, 2 * NDB:3 * NDB],
                                 in_=i["Dp"][dl]))
        self.qi(self.q.dma_start(out=b["pp"][:, 0:8], in_=i["nw"][dl]))
        self.qi(self.q.dma_start(out=b["pp"][:, 8:16], in_=i["nb"][dl]))
        m_dtw = self.qi(self.q.dma_start(out=b["rc"][0:64, 2048:2048 + DIN],
                                         in_=i["dtw"][dl]))
        m_xpa = self.qi(self.q.dma_start(
            out=b["rc"][:, 0:16 * 80], in_=i["xpwa"][dl]))
        m_xpb = self.qi(self.q.dma_start(
            out=b["rc"][:, 1280:1280 + 256], in_=i["xpwb"][dl]))
        self.wait("scalar", "LD", m_alog)
        ma = self.ac(self.a.activation(out=b["A"][:], in_=b["A"][:],
                                       func=AF.Exp))
        self.wait("vector", "AD", ma)
        self.vc(self.v.tensor_scalar_mul(out=b["A"][:], in0=b["A"][:],
                                         scalar1=-1.0))
        self.wait("vector", "LD", m_dtw)
        self.vi(self.v.tensor_copy(out=b["dtws"][:], in_=b["rc"][0:64, 2048:2048 + DIN]))
        self.wait("vector", "LD", m_xpa)
        self.vi(self.v.tensor_copy(out=b["xpa"][:], in_=b["rc"][:, 0:1280]))
        self.wait("vector", "LD", m_xpb)
        self.vi(self.v.tensor_copy(out=b["xpb"][:],
                                   in_=b["rc"][:, 1280:1280 + 256]))
        self.pi(self.g.memset(b["st"][:], 0.0))
        self.pi(self.g.memset(b["ut"][:], 0.0))
        self.pi(self.g.memset(b["dA0"][:], 0.0))
        self.pi(self.g.memset(b["dA1"][:], 0.0))
        self.barrier()

        chunks = range(self.NCH) if fw else range(self.NCH - 1, -1, -1)
        for c in chunks:
            self.mamba_chunk(dl, c, fw, src, dst)
            self.barrier()

    def u_ap(self, db, fw):
        off = 3 if fw else 0
        return self.b["u0"][:, db * 516 + off:db * 516 + off + self.T]

    def mamba_chunk(self, dl, c, fw, src, dst):
        b, i, s, p = self.b, self.i, self.s, self.p
        T = self.T
        cw0 = c * T
        m_rc = self.qi(self.q.dma_start(out=b["rc"][:], in_=s[src][c]))
        self.wait("scalar", "LD", m_rc)
        self.wait("vector", "LD", m_rc)
        self.wait("tensor", "LD", m_rc)
        gam = [b["pp"][:, k:k + 1] for k in range(8)]
        bet = [b["pp"][:, 8 + k:8 + k + 1] for k in range(8)]
        m_hn = self.layernorm(b["rc"][:], gamma=gam, beta=bet,
                              out_bf16=b["hn"][:, 0:NDO * T], nsec=NDO)

        u0_off = 3 if fw else 0
        mk_xz = {}
        m_evac = []
        for pr in range(NDB // 2):
            ring = b["wru"][:, (pr % 2) * 2048:(pr % 2) * 2048 + 2048]
            if pr >= 2:
                self.wait("sync", "TD", mk_xz[2 * pr - 3])
            m_w = self.qi(self.q.dma_start(
                out=ring, in_=s["wc_inwu"][dl][:, pr * 2048:(pr + 1) * 2048]))
            self.wait("tensor", "LD", m_w)
            self.wait("tensor", "VD", m_hn)
            for sub in range(2):
                dio = 2 * pr + sub
                pb = p[f"pb{2 + dio % 2}"]
                if dio >= 2:
                    self.wait("tensor", "AD", m_evac[dio - 2])
                for kb in range(8):
                    mt = self.ti(self.t.matmul(
                        pb[:], ring[:, sub * 1024 + kb * 128:
                                    sub * 1024 + (kb + 1) * 128],
                        b["hn"][:, kb * T:(kb + 1) * T],
                        start=(kb == 0), stop=(kb == 7)))
                mk_xz[dio] = mt
                self.wait("scalar", "TD", mt)
                me = self.ai(self.a.copy(
                    out=b["u0"][:, dio * 516 + u0_off:dio * 516 + u0_off + T],
                    in_=pb[:]))
                m_evac.append(me)

        tv = []
        for dio in range(NDB):
            self.wait("vector", "AD", m_evac[dio])
            if fw:
                self.vc(self.v.tensor_copy(
                    out=b["u0"][:, dio * 516:dio * 516 + 3],
                    in_=b["ut"][:, dio * 4:dio * 4 + 3]))
                mv = self.vc(self.v.tensor_copy(
                    out=b["ut"][:, dio * 4:dio * 4 + 3],
                    in_=b["u0"][:, dio * 516 + T:dio * 516 + T + 3]))
            else:
                self.vc(self.v.tensor_copy(
                    out=b["u0"][:, dio * 516 + T:dio * 516 + T + 3],
                    in_=b["ut"][:, dio * 4:dio * 4 + 3]))
                mv = self.vc(self.v.tensor_copy(
                    out=b["ut"][:, dio * 4:dio * 4 + 3],
                    in_=b["u0"][:, dio * 516:dio * 516 + 3]))
            tv.append(mv)

        m_u = []
        for db in range(NDB):
            if db >= 1:
                self.wait("vector", "AD", m_u[db - 1])
            for k in range(4):
                mdg = self.vi(self.v.tensor_scalar_mul(
                    out=b["diag"][:, k * 128:(k + 1) * 128], in0=b["ident"][:],
                    scalar1=b["cws"][:, db * 4 + k:db * 4 + k + 1]))
            self.wait("tensor", "VD", tv[db])
            self.wait("tensor", "VD", mdg)
            pb = p[f"pb{4 + db % 2}"]
            if db >= 2:
                self.wait("tensor", "AD", m_u[db - 2])
            for k in range(4):
                off = k if fw else (3 - k)
                mt = self.ti(self.t.matmul(
                    pb[:], b["diag"][:, k * 128:(k + 1) * 128],
                    b["u0"][:, db * 516 + off:db * 516 + off + T],
                    start=(k == 0), stop=(k == 3)))
            self.wait("scalar", "TD", mt)
            me = self.ai(self.a.activation(
                out=self.u_ap(db, fw), in_=pb[:], func=AF.Silu,
                bias=b["pq"][:, NDB + db:NDB + db + 1]))
            m_u.append(me)

        mk_z = {}
        m_z = []
        for pr in range(NDB // 2):
            ring = b["wrz"][:, (pr % 2) * 2048:(pr % 2) * 2048 + 2048]
            if pr >= 2:
                self.wait("sync", "TD", mk_z[2 * pr - 3])
            m_w = self.qi(self.q.dma_start(
                out=ring, in_=s["wc_inwz"][dl][:, pr * 2048:(pr + 1) * 2048]))
            self.wait("tensor", "LD", m_w)
            for sub in range(2):
                db = 2 * pr + sub
                pb = p[f"pb{db % 2}"]
                if db >= 2:
                    self.wait("tensor", "AD", m_z[db - 2])
                for kb in range(8):
                    mt = self.ti(self.t.matmul(
                        pb[:], ring[:, sub * 1024 + kb * 128:
                                    sub * 1024 + (kb + 1) * 128],
                        b["hn"][:, kb * T:(kb + 1) * T],
                        start=(kb == 0), stop=(kb == 7)))
                mk_z[db] = mt
                self.wait("scalar", "TD", mt)
                me = self.ai(self.a.activation(
                    out=b["zy"][:, db * T:(db + 1) * T],
                    in_=pb[:], func=AF.Silu))
                m_z.append(me)

        self.wait("tensor", "AD", m_u[-1])
        for kb in range(NDB):
            self.ti(self.t.matmul(
                p["pb2"][0:80, :], b["xpa"][:, kb * 80:(kb + 1) * 80],
                self.u_ap(kb, fw), start=(kb == 0), stop=(kb == NDB - 1)))
        for kb in range(NDB):
            mtb = self.ti(self.t.matmul(
                p["pb3"][0:16, :], b["xpb"][:, kb * 16:(kb + 1) * 16],
                self.u_ap(kb, fw), start=(kb == 0), stop=(kb == NDB - 1)))
        self.wait("scalar", "TD", mtb)
        self.ai(self.a.copy(out=b["dtr"][:], in_=p["pb2"][0:64, :]))
        self.ai(self.a.copy(out=b["brow"][:], in_=p["pb2"][64:80, :]))
        m_bc = self.ac(self.a.copy(out=b["crow"][:], in_=p["pb3"][0:16, :]))
        self.wait("gpsimd", "AD", m_bc)
        fl_b = s["bflat"][0:1, 2:2 + S * PITCH].rearrange(
            "o (s j) -> o s j", j=PITCH)[:, :, 0:T]
        fl_c = s["cflat"][0:1, 2:2 + S * PITCH].rearrange(
            "o (s j) -> o s j", j=PITCH)[:, :, 0:T]
        self.gi(self.g.dma_start(out=fl_b.squeeze(0), in_=b["brow"][:]))
        m_fl = self.gi(self.g.dma_start(out=fl_c.squeeze(0), in_=b["crow"][:]))
        self.wait("gpsimd", "GD", m_fl)
        self.gi(self.g.dma_start(
            out=b["bsl"][:], in_=s["bflat"].broadcast_to([128, S * PITCH + 2])))
        m_sl = self.gi(self.g.dma_start(
            out=b["csl"][:], in_=s["cflat"].broadcast_to([128, S * PITCH + 2])))

        mk = {"sp": {}, "dA": {}, "uDp": {}, "gate": {}, "dtmm": {}}
        for db in range(NDB):
            self.dblk(c, fw, db, mk, m_sl, m_bc)

        mk_ow = {}
        mres = []
        for pr in range(NDO // 2):
            ring = b["wro"][:, 0:4096]
            if pr >= 1:
                self.wait("sync", "TD", mk_ow[2 * pr - 1])
            m_w = self.qi(self.q.dma_start(
                out=ring, in_=s["wc_outw"][dl][:, pr * 4096:(pr + 1) * 4096]))
            self.wait("tensor", "LD", m_w)
            self.wait("tensor", "VD", mk["gate"][NDB - 1])
            for sub in range(2):
                do = 2 * pr + sub
                pb = p[f"pb{do % 2}"]
                if do >= 2:
                    self.wait("tensor", "VD", mres[do - 2])
                for kb in range(NDB):
                    mt = self.ti(self.t.matmul(
                        pb[:], ring[:, sub * 2048 + kb * 128:
                                    sub * 2048 + (kb + 1) * 128],
                        b["zy"][:, kb * T:(kb + 1) * T],
                        start=(kb == 0), stop=(kb == NDB - 1)))
                mk_ow[do] = mt
                self.wait("vector", "TD", mt)
                mv = self.vi(self.v.tensor_tensor(
                    out=b["rc"][:, do * T:(do + 1) * T],
                    in0=b["rc"][:, do * T:(do + 1) * T], in1=pb[:],
                    op=ALU.add))
                mres.append(mv)
        self.wait("gpsimd", "VD", mres[-1])
        self.gi(self.g.dma_start(out=s[dst][c], in_=b["rc"][:]))

    def dblk(self, c, fw, db, mk, m_sl, m_bc):
        b, p = self.b, self.p
        T = self.T
        par = db % 2
        dA = b["dA0"] if par == 0 else b["dA1"]
        pb_dt = p[f"pb{2 + par}"]
        dt = b["dt"][:, par * T:(par + 1) * T]

        if db < 2:
            self.wait("tensor", "AD", m_bc)
        else:
            self.wait("tensor", "AD", mk["sp"][db - 2])
        mt = self.ti(self.t.matmul(pb_dt[:],
                                   b["dtws"][:, db * 128:(db + 1) * 128],
                                   b["dtr"][:], start=True, stop=True))
        mk["dtmm"][db] = mt

        self.wait("scalar", "TD", mt)
        if db >= 2:
            self.wait("scalar", "VD", mk["uDp"][db - 2])
        self.ac(self.a.activation(out=dt, in_=pb_dt[:], func=AF.Exp,
                                  bias=b["pq"][:, db:db + 1]))
        msp = self.ac(self.a.activation(out=dt, in_=dt, func=AF.Ln,
                                        bias=b["kons"][:, 1:2]))
        mk["sp"][db] = msp
        for sct in range(S):
            o_ap = dA[:, sct * PITCH + 2:sct * PITCH + 2 + T]
            if not fw:
                o_ap = o_ap[:, ::-1]
            mda = self.ai(self.a.activation(
                out=o_ap, in_=dt, func=AF.Exp,
                scale=b["A"][:, db * S + sct:db * S + sct + 1]))
        mk["dA"][db] = mda

        self.wait("vector", "AD", mk["sp"][db])
        self.vc(self.v.tensor_tensor(out=b["dtu"][:, 2:2 + T], in0=dt,
                                     in1=self.u_ap(db, fw), op=ALU.mult))
        self.wait("vector", "GD", m_sl)
        if fw:
            in0 = b["dtu"][:, 0:PITCH].unsqueeze(1).broadcast_to(
                [128, S, PITCH])
            in1 = b["bsl"][:, 0:S * PITCH].rearrange("p (s j) -> p s j",
                                                     j=PITCH)
        else:
            in0 = b["dtu"][:, 2:516][:, ::-1].unsqueeze(1).broadcast_to(
                [128, S, PITCH])
            in1 = b["bsl"][:, 2:2 + S * PITCH].rearrange(
                "p (s j) -> p s j", j=PITCH)[:, :, ::-1]
        self.vc(self.v.tensor_tensor(
            out=b["qq"][:].rearrange("p (s j) -> p s j", j=PITCH),
            in0=in0, in1=in1, op=ALU.mult))
        self.vc(self.v.tensor_copy(
            out=b["qq"][:].rearrange("p (s j) -> p s j", j=PITCH)[:, :, 1],
            in_=b["st"][:, db * S:(db + 1) * S]))
        self.wait("vector", "AD", mk["dA"][db])
        self.vc(self.v.tensor_tensor_scan(
            out=b["h"][:], data0=dA[:], data1=b["qq"][:], initial=0.0,
            op0=ALU.mult, op1=ALU.add))
        self.vc(self.v.tensor_copy(
            out=b["st"][:, db * S:(db + 1) * S],
            in_=b["h"][:].rearrange("p (s j) -> p s j",
                                    j=PITCH)[:, :, PITCH - 1]))
        g3 = dA[:].rearrange("p (s j) -> p s j", j=PITCH)[:, :, 2:PITCH]
        h3 = b["h"][:].rearrange("p (s j) -> p s j", j=PITCH)[:, :, 2:PITCH]
        c3 = b["csl"][:, 0:S * PITCH].rearrange("p (s j) -> p s j",
                                                j=PITCH)[:, :, 2:PITCH]
        if not fw:
            c3 = c3[:, :, ::-1]
        self.vc(self.v.tensor_tensor(out=g3, in0=h3, in1=c3, op=ALU.mult))
        for w in (8, 4, 2, 1):
            self.vc(self.v.tensor_tensor(
                out=dA[:, 0:w * PITCH], in0=dA[:, 0:w * PITCH],
                in1=dA[:, w * PITCH:2 * w * PITCH], op=ALU.add))
        ytree = dA[:, 2:2 + T]
        if not fw:
            ytree = ytree[:, ::-1]
        muv = self.vc(self.v.scalar_tensor_tensor(
            out=b["dtu"][:, 2:2 + T], in0=self.u_ap(db, fw),
            scalar=b["pq"][:, 2 * S + db:2 * S + db + 1], in1=ytree,
            op0=ALU.mult, op1=ALU.add))
        mk["uDp"][db] = muv
        mg = self.vc(self.v.tensor_tensor(
            out=b["zy"][:, db * T:(db + 1) * T], in0=b["dtu"][:, 2:2 + T],
            in1=b["zy"][:, db * T:(db + 1) * T], op=ALU.mult))
        mk["gate"][db] = mg

    def ffn_phase(self, p3):
        b, i, s, p = self.b, self.i, self.s, self.p
        T = self.T
        for c in range(self.NCH):
            cw0 = c * T
            if not p3:
                m_rc = self.qi(self.q.dma_start(out=b["rc"][:], in_=i["xT"][c]))
                for e in ("vector", "scalar", "tensor"):
                    self.wait(e, "LD", m_rc)
            else:
                nfw = [b["pg"][:, k:k + 1] for k in range(NDO)]
                nfb_ = [b["pg"][:, NDO + k:NDO + k + 1] for k in range(NDO)]
                m_f = self.qi(self.q.dma_start(out=b["rc"][:], in_=s["fs"][c]))
                for e in ("vector", "scalar", "tensor"):
                    self.wait(e, "LD", m_f)
                self.layernorm(b["rc"][:], gamma=nfw, beta=nfb_,
                               out_bf16=b["hn"][:, 0:NDO * T], nsec=NDO)
                self.wait("sync", "VD")
                m_b = self.qi(self.q.dma_start(out=b["rc"][:], in_=s["bs"][c]))
                for e in ("vector", "scalar", "tensor"):
                    self.wait(e, "LD", m_b)
                m_hn = self.layernorm(b["rc"][:], gamma=nfw, beta=nfb_,
                                      out_bf16=b["zy"][:, 0:NDO * T],
                                      nsec=NDO)
                self.wait("sync", "VD", m_hn)
                pass
                m_a = self.qi(self.q.dma_start(out=b["rc"][:], in_=s["a"][c]))
                mres = []
                mlast = {}
                for do in range(NDO):
                    pb = p[f"pb{do % 2}"]
                    ring = b["wro"][:, (do % 2) * 2048:(do % 2) * 2048 + 2048]
                    if do >= 2:
                        self.wait("sync", "TD", mlast[do - 1])
                    m_w = self.qi(self.q.dma_start(
                        out=ring, in_=s["wc_mg"][:, do * 2048:(do + 1) * 2048]))
                    self.wait("tensor", "LD", m_w)
                    self.wait("tensor", "VD", m_hn)
                    if do >= 2:
                        self.wait("tensor", "VD", mres[do - 2])
                    for kb in range(16):
                        rhs = (b["hn"][:, kb * T:(kb + 1) * T] if kb < 8 else
                               b["zy"][:, (kb - 8) * T:(kb - 7) * T])
                        mt = self.ti(self.t.matmul(
                            pb[:], ring[:, kb * 128:(kb + 1) * 128], rhs,
                            start=(kb == 0), stop=(kb == 15)))
                    mlast[do] = mt
                    self.wait("vector", "LD", m_a)
                    self.wait("vector", "TD", mt)
                    self.vc(self.v.tensor_scalar_add(
                        out=b["t2"][:], in0=pb[:],
                        scalar1=b["pg"][:, 2 * NDO + do:2 * NDO + do + 1]))
                    mv = self.vc(self.v.tensor_tensor(
                        out=b["rc"][:, do * T:(do + 1) * T],
                        in0=b["rc"][:, do * T:(do + 1) * T], in1=b["t2"][:],
                        op=ALU.add))
                    mres.append(mv)
                for e in ("scalar", "tensor"):
                    self.wait(e, "VD", mres[-1])

            stage = self.cfg.get("stage")
            vv = b["dA0"][:].bitcast(F32)[:, 0:NDO * T]
            m_v = self.layernorm(b["rc"][:], out_f32=vv, nsec=NDO)
            if stage == "ln":
                self.wait("gpsimd", "VD", m_v)
                self.gi(self.g.dma_start(out=i["out"][c], in_=vv))
                self.barrier()
                continue
            gam = [b["pf"][:, NFB + NDO + k:NFB + NDO + k + 1]
                   for k in range(NDO)]
            bet = [b["pf"][:, NFB + 2 * NDO + k:NFB + 2 * NDO + k + 1]
                   for k in range(NDO)]
            self.wait("scalar", "VD", m_v)
            self.wait("tensor", "VD", m_v)
            m_w = self.layernorm(vv, gamma=gam, beta=bet,
                                 out_bf16=b["hn"][:, 0:NDO * T], nsec=NDO,
                                 pbA="pb2", pbB="pb3")
            if stage == "lng":
                self.wait("vector", "VD", m_w)
                mcp = self.vi(self.v.tensor_copy(out=vv, in_=b["hn"][:, 0:NDO * T]))
                self.wait("gpsimd", "VD", mcp)
                self.gi(self.g.dma_start(out=i["out"][c], in_=vv))
                self.barrier()
                continue
            mid = {}
            m_mid = []
            mk1 = {}
            for pr in range(NFB // 2):
                ring = b["wru"][:, (pr % 2) * 2048:(pr % 2) * 2048 + 2048]
                if pr >= 2:
                    self.wait("sync", "TD", mk1[2 * pr - 3])
                m_wt = self.qi(self.q.dma_start(
                    out=ring, in_=s["wc_w1"][:, pr * 2048:(pr + 1) * 2048]))
                self.wait("tensor", "LD", m_wt)
                self.wait("tensor", "VD", m_w)
                for sub in range(2):
                    f = 2 * pr + sub
                    pb = p[f"pb{4 + f % 2}"]
                    if f >= 2:
                        self.wait("tensor", "AD", m_mid[f - 2])
                    for kb in range(8):
                        mt = self.ti(self.t.matmul(
                            pb[:], ring[:, sub * 1024 + kb * 128:
                                        sub * 1024 + (kb + 1) * 128],
                            b["hn"][:, kb * T:(kb + 1) * T],
                            start=(kb == 0), stop=(kb == 7)))
                    mk1[f] = mt
                    self.wait("scalar", "TD", mt)
                    arena = b["dA1"] if f < 16 else b["qq"]
                    fo = (f % 16) * T
                    mid[f] = (arena, fo)
                    me = self.ai(self.a.activation(
                        out=arena[:, fo:fo + T], in_=pb[:], func=AF.Silu,
                        bias=b["pf"][:, f:f + 1]))
                    m_mid.append(me)
            if stage == "fc1":
                self.wait("vector", "AD", m_mid[-1])
                mcp = self.vi(self.v.tensor_copy(out=vv, in_=b["dA1"][:, 0:NDO * T]))
                self.wait("gpsimd", "VD", mcp)
                self.gi(self.g.dma_start(out=i["out"][c], in_=vv))
                self.barrier()
                continue
            if self.cfg.get("bar_fc"):
                self.barrier()
            mout = []
            for do in range(NDO):
                ring2 = b["wrz"][:, (do % 2) * 2048:(do % 2) * 2048 + 2048]
                pb = p[f"pb{do % 2}"]
                for half in range(2):
                    m_wt = self.qi(self.q.dma_start(
                        out=ring2,
                        in_=s["wc_w2"][:, do * 4096 + half * 2048:
                                       do * 4096 + (half + 1) * 2048]))
                    self.wait("tensor", "LD", m_wt)
                    if half == 0:
                        self.wait("tensor", "AD", m_mid[-1])
                        if do >= 2:
                            self.wait("tensor", "VD", mout[do - 2])
                    for kb in range(16):
                        ff = half * 16 + kb
                        ar, fo = mid[ff]
                        mt = self.ti(self.t.matmul(
                            pb[:], ring2[:, kb * 128:(kb + 1) * 128],
                            ar[:, fo:fo + T],
                            start=(half == 0 and kb == 0),
                            stop=(half == 1 and kb == 15)))
                    self.wait("sync", "TD", mt)
                self.wait("vector", "TD", mt)
                self.vc(self.v.tensor_copy(out=b["t2"][:, 0:256],
                                           in_=b["t2"][:, 0:256]))
                self.vc(self.v.tensor_scalar(
                    out=b["t2"][:], in0=pb[:],
                    scalar1=b["pf"][:, NFB + do:NFB + do + 1], scalar2=0.25,
                    op0=ALU.add, op1=ALU.mult))
                self.vc(self.v.tensor_tensor(
                    out=b["rc"][:, do * T:(do + 1) * T],
                    in0=b["rc"][:, do * T:(do + 1) * T], in1=b["t2"][:],
                    op=ALU.add))
                mv = self.vi(self.v.scalar_tensor_tensor(
                    out=b["rc"][:, do * T:(do + 1) * T],
                    in0=vv[:, do * T:(do + 1) * T], scalar=0.5,
                    in1=b["rc"][:, do * T:(do + 1) * T],
                    op0=ALU.mult, op1=ALU.add))
                mout.append(mv)
            if stage == "fc2":
                self.wait("gpsimd", "VD", mout[-1])
                self.gi(self.g.dma_start(out=i["out"][c], in_=b["rc"][:]))
                self.barrier()
                continue
            self.wait("gpsimd", "VD", mout[-1])
            dst = i["out"] if p3 else s["a"]
            self.gi(self.g.dma_start(out=dst[c], in_=b["rc"][:]))
            if not p3:
                for e in ("vector", "scalar", "tensor"):
                    self.wait(e, "VD", mout[-1])
                m_x = self.layernorm(b["rc"][:], out_f32=vv, nsec=NDO,
                                     pbA="pb2", pbB="pb3")
                self.wait("gpsimd", "VD", m_x)
                self.gi(self.g.dma_start(out=s["xh"][c], in_=vv))
            self.barrier()


def _prep_inputs(inputs, cfg):
    L = cfg["L"]
    T = cfg["T"]
    NL = cfg["NLAYER"]
    f = np.float32

    def pmajor(wT):
        K, N = wT.shape
        return np.ascontiguousarray(
            wT.reshape(K // 128, 128, N // 128, 128)
            .transpose(1, 2, 0, 3).reshape(128, -1))

    def colpack(vals):
        n = vals.shape[0]
        nb_ = n // 128
        return np.ascontiguousarray(
            vals.reshape(nb_, 128, -1).transpose(1, 0, 2).reshape(128, -1))

    d = {}
    x = np.asarray(inputs["L0_attn_x"], dtype=f)
    B = x.shape[0]

    def lay(name):
        fwv = np.asarray(inputs[f"L0_attn_fw_{name}"], dtype=f)
        bwv = np.asarray(inputs[f"L0_attn_bw_{name}"], dtype=f)
        return [fwv[l] for l in range(NL)] + [bwv[l] for l in range(NL)]

    inW = lay("in_W")
    d["inwu"] = np.stack([pmajor(np.ascontiguousarray(w[:DIN].T))
                          for w in inW]).astype(f)
    d["inwz"] = np.stack([pmajor(np.ascontiguousarray(w[DIN:].T))
                          for w in inW]).astype(f)
    d["outw"] = np.stack([pmajor(np.ascontiguousarray(w.T))
                          for w in lay("out_W")]).astype(f)
    d["xpwa"] = np.stack([np.ascontiguousarray(
        w.T[:, 0:80].reshape(16, 128, 80).transpose(1, 0, 2).reshape(128, -1))
        for w in lay("xproj_W")]).astype(f)
    d["xpwb"] = np.stack([np.ascontiguousarray(
        w.T[:, 80:96].reshape(16, 128, 16).transpose(1, 0, 2).reshape(128, -1))
        for w in lay("xproj_W")]).astype(f)
    d["dtw"] = np.stack([np.ascontiguousarray(w.T)
                         for w in lay("dt_W")]).astype(f)
    d["alog"] = np.stack([colpack(v) for v in lay("Alog")]).astype(f)
    d["cw"] = np.stack([colpack(v) for v in lay("conv_w")]).astype(f)
    d["cb"] = np.stack([colpack(v[:, None]) for v in lay("conv_b")]).astype(f)
    d["dtb"] = np.stack([colpack(v[:, None]) for v in lay("dt_b")]).astype(f)
    d["Dp"] = np.stack([colpack(v[:, None]) for v in lay("D")]).astype(f)
    d["nw"] = np.stack([colpack(v[:, None]) for v in lay("norm_w")]).astype(f)
    d["nb"] = np.stack([colpack(v[:, None]) for v in lay("norm_b")]).astype(f)
    d["fc1q"] = pmajor(np.asarray(inputs["L0_ffn_fc1_q"])).astype(np.int32)
    d["fc2q"] = pmajor(np.asarray(inputs["L0_ffn_fc2_q"])).astype(np.int32)
    s1 = np.asarray(inputs["L0_ffn_fc1_scales"], dtype=f)
    d["fc1s"] = np.ascontiguousarray(np.broadcast_to(
        s1.reshape(8, NFB, 128).transpose(1, 0, 2)[None],
        (128, NFB, 8, 128)).reshape(128, -1))
    s2 = np.asarray(inputs["L0_ffn_fc2_scales"], dtype=f)
    d["fc2s"] = np.ascontiguousarray(np.broadcast_to(
        s2.reshape(NFB, NDO, 128).transpose(1, 0, 2)[None],
        (128, NDO, NFB, 128)).reshape(128, -1))
    d["b1"] = colpack(np.asarray(inputs["L0_ffn_fc1_bias"], f)[:, None])
    d["b2"] = colpack(np.asarray(inputs["L0_ffn_fc2_bias"], f)[:, None])
    d["lng"] = colpack(np.asarray(inputs["L0_ffn_ln_gamma"], f)[:, None])
    d["lnb"] = colpack(np.asarray(inputs["L0_ffn_ln_beta"], f)[:, None])
    d["nfw"] = colpack(np.asarray(inputs["L0_attn_normf_w"], f)[:, None])
    d["nfb"] = colpack(np.asarray(inputs["L0_attn_normf_b"], f)[:, None])
    d["mgw"] = pmajor(np.ascontiguousarray(
        np.asarray(inputs["L0_attn_merge_W"], f).T)).astype(f)
    d["mgb"] = colpack(np.asarray(inputs["L0_attn_merge_b"], f)[:, None])
    ow = np.zeros((128, 2), f)
    ow[:, 0] = 1.0 / 1024.0
    ow[:, 1] = 1.0
    d["onesw"] = ow
    d["onescol"] = np.ones((1, 128), f)
    kons = np.zeros((128, 2), f)
    kons[:, 0] = 1e-5
    kons[:, 1] = 1.0
    d["kons"] = kons
    d["ident"] = np.eye(128).astype(ml_dtypes.bfloat16)

    in_maps = []
    for bb in range(B):
        m = dict(d)
        xt = x[bb, :L].reshape(L // T, T, NDO, 128)
        m["xT"] = np.ascontiguousarray(
            xt.transpose(0, 3, 2, 1)).reshape(L // T, 128, NDO * T).astype(f)
        in_maps.append(m)
    return in_maps


def _run_raw(inputs, cfg):
    in_maps = _prep_inputs(inputs, cfg)
    nc = bass.Bass("TRN2", target_bir_lowering=False, debug=False)
    mk = MK(nc, cfg)
    mk.build()
    return run_bass_kernel_spmd(nc, in_maps, core_ids=list(range(len(in_maps))),
                                trace_cores=[0])


def kernel(**inputs):
    cfg = dict(CFG)
    res = _run_raw(inputs, cfg)
    outs = []
    T = cfg["T"]
    for r in res.results:
        o = np.asarray(r["out"]).reshape(cfg["L"] // T, 128, NDO, T)
        outs.append(np.ascontiguousarray(
            o.transpose(0, 3, 2, 1).reshape(cfg["L"], D)))
    return np.stack(outs).astype(np.float32)


# version 07 (final)
# speedup vs baseline: 1.0014x; 1.0014x over previous
import numpy as np
import ml_dtypes
import concourse.bass as bass
import concourse.mybir as mybir
from concourse.bass_utils import run_bass_kernel_spmd
from contextlib import ExitStack

F32 = mybir.dt.float32
BF16 = mybir.dt.bfloat16
I32 = mybir.dt.int32
AF = mybir.ActivationFunctionType
ALU = mybir.AluOpType

CFG = dict(L=8192, T=512, NLAYER=4)
D = 1024
DIN = 2048
S = 16
FF = 4096
NDB = DIN // 128
NDO = D // 128
NFB = FF // 128
PITCH = 514
EPS = 1e-5


class _SinkInst:
    def then_inc(self, *a, **k):
        return self


_SINKINST = _SinkInst()


class _Sink:
    def __getattr__(self, k):
        return lambda *a, **kw: _SINKINST


_SINK = _Sink()


class MK:
    def __init__(self, nc, cfg):
        self.nc = nc
        self.cfg = cfg
        self.L = cfg["L"]
        self.T = cfg["T"]
        self.NCH = self.L // self.T
        self.NL = cfg["NLAYER"]

    def E(self, name):
        return self.eng if self.engine == name else _SINK

    @property
    def v(self):
        return self.E("vector")

    @property
    def a(self):
        return self.E("scalar")

    @property
    def t(self):
        return self.E("tensor")

    @property
    def g(self):
        return self.E("gpsimd")

    @property
    def q(self):
        return self.E("sync")

    def inc(self, name, inst, sem, n=1):
        if self.engine == name:
            inst.then_inc(self.sems[sem], n)
        self.c[sem] += n
        return self.c[sem]

    def wait(self, name, sem, val=None):
        if self.engine == name:
            self.eng.wait_ge(self.sems[sem], self.c[sem] if val is None else val)

    def vi(self, inst, n=1):
        return self.inc("vector", inst, "VD", n)

    def ai(self, inst, n=1):
        return self.inc("scalar", inst, "AD", n)

    def ti(self, inst, n=1):
        return self.inc("tensor", inst, "TD", n)

    def qi(self, inst):
        m = self.inc("sync", inst, "LD", 16)
        self.wait("sync", "LD", m)
        return m

    def gi(self, inst):
        m = self.inc("gpsimd", inst, "GD", 16)
        self.wait("gpsimd", "GD", m)
        return m

    def pi(self, inst, n=1):
        return self.inc("gpsimd", inst, "PD", n)

    def vc(self, inst):
        m = self.vi(inst)
        self.wait("vector", "VD", m)
        return m

    def ac(self, inst):
        m = self.ai(inst)
        self.wait("scalar", "AD", m)
        return m

    def run(self, engine, eng):
        self.engine, self.eng = engine, eng
        self.c = {k: 0 for k in self.sems}
        self.script()

    def barrier(self):
        for e in ("sync", "vector", "scalar", "tensor", "gpsimd"):
            for s in self.sems:
                self.wait(e, s)

    def script(self):
        if self.cfg.get("stage"):
            self.prologue()
            self.barrier()
            self.ffn_phase(p3=False)
            self.barrier()
            return
        self.prologue()
        self.barrier()
        self.ffn_phase(p3=False)
        for dl in range(2 * self.NL):
            self.mamba_layer(dl)
        self.ffn_phase(p3=True)
        self.barrier()
        if self.cfg.get("dump"):
            for nm in ("a", "xh", "fs", "bs", "rsA", "rsB"):
                self.gi(self.g.dma_start(out=self.i["d_" + nm],
                                         in_=self.s[nm]))
            self.barrier()

    def build(self):
        nc = self.nc
        L, T = self.L, self.T
        NDL = 2 * self.NL

        def dram(name, shape, dt, kind="ExternalInput"):
            return nc.dram_tensor(name, shape, dt, kind=kind).ap()

        i = {}
        i["xT"] = dram("xT", [L // T, 128, NDO * T], F32)
        i["inwu"] = dram("inwu", [NDL, 128, NDB * 8 * 128], F32)
        i["inwz"] = dram("inwz", [NDL, 128, NDB * 8 * 128], F32)
        i["outw"] = dram("outw", [NDL, 128, NDO * NDB * 128], F32)
        i["xpwa"] = dram("xpwa", [NDL, 128, 16 * 80], F32)
        i["xpwb"] = dram("xpwb", [NDL, 128, 16 * 16], F32)
        i["dtw"] = dram("dtw", [NDL, 64, DIN], F32)
        i["alog"] = dram("alog", [NDL, 128, NDB * S], F32)
        i["cw"] = dram("cw", [NDL, 128, NDB * 4], F32)
        i["cb"] = dram("cb", [NDL, 128, NDB], F32)
        i["dtb"] = dram("dtb", [NDL, 128, NDB], F32)
        i["Dp"] = dram("Dp", [NDL, 128, NDB], F32)
        i["nw"] = dram("nw", [NDL, 128, NDO], F32)
        i["nb"] = dram("nb", [NDL, 128, NDO], F32)
        i["fc1q"] = dram("fc1q", [128, NFB * 8 * 128], I32)
        i["fc2q"] = dram("fc2q", [128, NDO * NFB * 128], I32)
        i["fc1s"] = dram("fc1s", [128, NFB * 8 * 128], F32)
        i["fc2s"] = dram("fc2s", [128, NDO * NFB * 128], F32)
        i["b1"] = dram("b1", [128, NFB], F32)
        i["b2"] = dram("b2", [128, NDO], F32)
        i["lng"] = dram("lng", [128, NDO], F32)
        i["lnb"] = dram("lnb", [128, NDO], F32)
        i["nfw"] = dram("nfw", [128, NDO], F32)
        i["nfb"] = dram("nfb", [128, NDO], F32)
        i["mgw"] = dram("mgw", [128, NDO * 16 * 128], F32)
        i["mgb"] = dram("mgb", [128, NDO], F32)
        i["onesw"] = dram("onesw", [128, 2], F32)
        i["onescol"] = dram("onescol", [1, 128], F32)
        i["ident"] = dram("ident", [128, 128], BF16)
        i["kons"] = dram("kons", [128, 2], F32)
        i["out"] = dram("out", [L // T, 128, NDO * T], F32, kind="ExternalOutput")
        if self.cfg.get("dump"):
            for nm in ("a", "xh", "fs", "bs", "rsA", "rsB"):
                i["d_" + nm] = dram("d_" + nm, [L // T, 128, NDO * T], F32,
                                    kind="ExternalOutput")
        self.i = i

        s = {}
        s["wc_inwu"] = dram("wc_inwu", [NDL, 128, NDB * 8 * 128], BF16, "Internal")
        s["wc_inwz"] = dram("wc_inwz", [NDL, 128, NDB * 8 * 128], BF16, "Internal")
        s["wc_outw"] = dram("wc_outw", [NDL, 128, NDO * NDB * 128], BF16, "Internal")
        s["wc_w1"] = dram("wc_w1", [128, NFB * 8 * 128], BF16, "Internal")
        s["wc_w2"] = dram("wc_w2", [128, NDO * NFB * 128], BF16, "Internal")
        s["wc_mg"] = dram("wc_mg", [128, NDO * 16 * 128], BF16, "Internal")
        s["xh"] = dram("sc_xh", [L // T, 128, NDO * T], F32, "Internal")
        s["a"] = dram("sc_a", [L // T, 128, NDO * T], F32, "Internal")
        s["rsA"] = dram("sc_rsA", [L // T, 128, NDO * T], F32, "Internal")
        s["rsB"] = dram("sc_rsB", [L // T, 128, NDO * T], F32, "Internal")
        s["fs"] = dram("sc_fs", [L // T, 128, NDO * T], F32, "Internal")
        s["bs"] = dram("sc_bs", [L // T, 128, NDO * T], F32, "Internal")
        s["bflat"] = dram("sc_bflat", [1, S * PITCH + 2], BF16, "Internal")
        s["cflat"] = dram("sc_cflat", [1, S * PITCH + 2], BF16, "Internal")
        self.s = s

        with ExitStack() as ctx:
            def sb(name, shape, dt):
                return ctx.enter_context(nc.sbuf_tensor(name, shape, dt))

            b = {}
            b["rc"] = sb("rc", [128, NDO * T], F32)
            b["hn"] = sb("hn", [128, NDO * T], BF16)
            b["u0"] = sb("u0", [128, NDB * 516], BF16)
            b["zy"] = sb("zy", [128, NDB * T], BF16)
            b["dA0"] = sb("dA0", [128, S * PITCH], BF16)
            b["dA1"] = sb("dA1", [128, S * PITCH], BF16)
            b["qq"] = sb("qq", [128, S * PITCH], BF16)
            b["h"] = sb("h", [128, S * PITCH], BF16)
            b["bsl"] = sb("bsl", [128, S * PITCH + 2], BF16)
            b["csl"] = sb("csl", [128, S * PITCH + 2], BF16)
            b["wru"] = sb("wru", [128, 2 * 2048], BF16)
            b["wrz"] = sb("wrz", [128, 2 * 2048], BF16)
            b["wro"] = sb("wro", [128, 4096], BF16)
            b["wrf"] = sb("wrf", [128, 1024], F32)
            b["wrb"] = sb("wrb", [128, 2048], BF16)
            b["dtws"] = sb("dtws", [64, DIN], BF16)
            b["dtr"] = sb("dtr", [64, T], BF16)
            b["brow"] = sb("brow", [16, T], BF16)
            b["crow"] = sb("crow", [16, T], BF16)
            b["dt"] = sb("dt", [128, 2 * T], BF16)
            b["dtu"] = sb("dtu", [128, 516], BF16)
            b["diag"] = sb("diag", [128, 4 * 128], BF16)
            b["xpa"] = sb("xpa", [128, 16 * 80], BF16)
            b["xpb"] = sb("xpb", [128, 16 * 16], BF16)
            b["st"] = sb("st", [128, NDB * S], BF16)
            b["A"] = sb("A", [128, NDB * S], F32)
            b["cws"] = sb("cws", [128, NDB * 4], F32)
            b["pp"] = sb("pp", [128, 16], F32)
            b["pq"] = sb("pq", [128, NDB * 3], F32)
            b["pf"] = sb("pf", [128, NFB + 3 * NDO], F32)
            b["pg"] = sb("pg", [128, 3 * NDO], F32)
            b["onesw"] = sb("onesws", [128, 2], F32)
            b["onescol"] = sb("onescols", [1, 128], F32)
            b["ident"] = sb("idents", [128, 128], BF16)
            b["kons"] = sb("konss", [128, 2], F32)
            b["t2"] = sb("t2", [128, T], F32)
            b["ut"] = sb("ut", [128, NDB * 4], BF16)
            self.b = b

            p = {}
            for k in range(6):
                p[f"pb{k}"] = ctx.enter_context(
                    nc.psum_tensor(f"pb{k}", [128, T], F32))
            p["ps1"] = ctx.enter_context(nc.psum_tensor("ps1", [1, T], F32))
            p["ps2"] = ctx.enter_context(nc.psum_tensor("ps2", [1, T], F32))
            self.p = p

            sems = {}
            for name in ("LD", "GD", "TD", "AD", "VD", "PD"):
                sems[name] = ctx.enter_context(nc.semaphore(name))
            self.sems = sems

            block = ctx.enter_context(nc.Block())

            @block.sync
            def _(e):
                self.run("sync", e)

            @block.vector
            def _(e):
                self.run("vector", e)

            @block.scalar
            def _(e):
                self.run("scalar", e)

            @block.tensor
            def _(e):
                self.run("tensor", e)

            @block.gpsimd
            def _(e):
                self.run("gpsimd", e)

        return nc

    def prologue(self):
        b, i, s = self.b, self.i, self.s
        NDL = 2 * self.NL
        self._pro = {"i": 0, "cast": [], "store": []}
        self.qi(self.q.dma_start(out=b["onesw"][:], in_=i["onesw"]))
        self.qi(self.q.dma_start(out=b["onescol"][:], in_=i["onescol"]))
        self.qi(self.q.dma_start(out=b["ident"][:], in_=i["ident"]))
        self.qi(self.q.dma_start(out=b["kons"][:], in_=i["kons"]))
        self.qi(self.q.dma_start(out=b["pf"][:, 0:NFB], in_=i["b1"]))
        self.qi(self.q.dma_start(out=b["pf"][:, NFB:NFB + NDO], in_=i["b2"]))
        self.qi(self.q.dma_start(out=b["pf"][:, NFB + NDO:NFB + 2 * NDO],
                                 in_=i["lng"]))
        self.qi(self.q.dma_start(out=b["pf"][:, NFB + 2 * NDO:NFB + 3 * NDO],
                                 in_=i["lnb"]))
        self.qi(self.q.dma_start(out=b["pg"][:, 0:NDO], in_=i["nfw"]))
        self.qi(self.q.dma_start(out=b["pg"][:, NDO:2 * NDO], in_=i["nfb"]))
        self.qi(self.q.dma_start(out=b["pg"][:, 2 * NDO:3 * NDO], in_=i["mgb"]))
        self.pi(self.g.memset(b["dtu"][:], 0.0))
        self.pi(self.g.memset(b["qq"][:], 0.0))
        self.pi(self.g.memset(b["h"][:], 0.0))
        m0 = self.pi(self.g.memset(b["bsl"][:], 0.0))
        self.wait("gpsimd", "PD", m0)
        self.gi(self.g.dma_start(out=s["bflat"],
                                 in_=b["bsl"][0:1, 0:S * PITCH + 2]))
        self.gi(self.g.dma_start(out=s["cflat"],
                                 in_=b["bsl"][0:1, 0:S * PITCH + 2]))

        for nm_src, nm_dst in (("inwu", "wc_inwu"), ("inwz", "wc_inwz")):
            for dl in range(NDL):
                for c0 in range(0, NDB * 1024, 1024):
                    self.cast_slab(i[nm_src][dl][:, c0:c0 + 1024],
                                   s[nm_dst][dl][:, c0:c0 + 1024], 1024)
        for dl in range(NDL):
            for c0 in range(0, NDO * 2048, 1024):
                self.cast_slab(i["outw"][dl][:, c0:c0 + 1024],
                               s["wc_outw"][dl][:, c0:c0 + 1024], 1024)
        for c0 in range(0, NDO * 2048, 1024):
            self.cast_slab(i["mgw"][:, c0:c0 + 1024],
                           s["wc_mg"][:, c0:c0 + 1024], 1024)
        for c0 in range(0, NFB * 1024, 1024):
            self.deq_slab(i["fc1q"][:, c0:c0 + 1024],
                          i["fc1s"][:, c0:c0 + 1024],
                          s["wc_w1"][:, c0:c0 + 1024])
        for c0 in range(0, NDO * NFB * 128, 1024):
            self.deq_slab(i["fc2q"][:, c0:c0 + 1024],
                          i["fc2s"][:, c0:c0 + 1024],
                          s["wc_w2"][:, c0:c0 + 1024])

    def cast_slab(self, src_ap, dst_ap, n):
        b = self.b
        wf3 = b["wrf"][:, 0:n].rearrange("p (k n) -> p k n", n=128)
        wb3 = b["wrb"][:, 0:n].rearrange("p (k n) -> p k n", n=128)
        self.wait("sync", "VD")
        m = self.qi(self.q.dma_start(out=wf3, in_=src_ap))
        self.wait("vector", "LD", m)
        self.wait("vector", "GD")
        mv = self.vi(self.v.tensor_copy(out=b["wrb"][:, 0:n],
                                        in_=b["wrf"][:, 0:n]))
        self.wait("gpsimd", "VD", mv)
        self.gi(self.g.dma_start(out=dst_ap, in_=wb3))

    def deq_slab(self, q_ap, s_ap, dst_ap):
        b = self.b
        n = 1024
        self.wait("sync", "VD")
        m = self.qi(self.q.dma_start(out=b["wrf"][:, 0:n].bitcast(I32),
                                     in_=q_ap))
        m2 = self.qi(self.q.dma_start(out=b["rc"][:, 0:n], in_=s_ap))
        self.wait("vector", "LD", m2)
        self.wait("vector", "GD")
        self.vc(self.v.tensor_copy(out=b["wrf"][:, 0:n],
                                   in_=b["wrf"][:, 0:n].bitcast(I32)))
        mv = self.vi(self.v.scalar_tensor_tensor(
            out=b["wrb"][:, 0:n], in0=b["wrf"][:, 0:n], scalar=-128.0,
            in1=b["rc"][:, 0:n], op0=ALU.add, op1=ALU.mult))
        self.wait("gpsimd", "VD", mv)
        self.gi(self.g.dma_start(
            out=dst_ap,
            in_=b["wrb"][:, 0:n].rearrange("p (k n) -> p k n", n=128)))

    def layernorm(self, src_f32_ap, gamma=None, beta=None, out_bf16=None,
                  out_f32=None, nsec=8, pbA="pb0", pbB="pb1"):
        b, p = self.b, self.p
        T = self.T
        sq = b["zy"][:].bitcast(F32)
        ma = self.ai(self.a.activation(out=sq[:, 0:nsec * T], in_=src_f32_ap,
                                       func=AF.Square))
        for k in range(nsec):
            mt1 = self.ti(self.t.matmul(p["ps1"][:], b["onesw"][:, 0:1],
                                        src_f32_ap[:, k * T:(k + 1) * T],
                                        start=(k == 0), stop=(k == nsec - 1)))
        self.wait("tensor", "AD", ma)
        for k in range(nsec):
            mt2 = self.ti(self.t.matmul(p["ps2"][:], b["onesw"][:, 0:1],
                                        sq[:, k * T:(k + 1) * T],
                                        start=(k == 0), stop=(k == nsec - 1)))
        rowsf = b["u0"][:].bitcast(F32)
        r_a = rowsf[0:1, 0:T]
        r_b = rowsf[0:1, T:2 * T]
        r_t = rowsf[0:1, 2 * T:3 * T]
        self.wait("scalar", "TD", mt1)
        ma = self.ac(self.a.activation(out=r_t, in_=p["ps1"][:], func=AF.Square))
        self.wait("vector", "TD", mt2)
        self.wait("vector", "AD", ma)
        mv = self.vc(self.v.tensor_tensor(out=r_t, in0=p["ps2"][:], in1=r_t,
                                          op=ALU.subtract))
        self.wait("scalar", "VD", mv)
        self.ac(self.a.activation(out=r_t, in_=r_t, func=AF.Ln,
                                  bias=b["kons"][0:1, 0:1]))
        ma = self.ac(self.a.activation(out=r_a, in_=r_t, func=AF.Exp,
                                       scale=-0.5))
        self.wait("vector", "AD", ma)
        mv = self.vc(self.v.tensor_tensor(out=r_b, in0=p["ps1"][:], in1=r_a,
                                          op=ALU.mult))
        self.wait("tensor", "VD", mv)
        self.ti(self.t.matmul(p[pbA][:], b["onescol"][:], r_a,
                              start=True, stop=True))
        mtb = self.ti(self.t.matmul(p[pbB][:], b["onescol"][:], r_b,
                                    start=True, stop=True))
        self.wait("vector", "TD", mtb)
        mv = 0
        for k in range(nsec):
            self.vc(self.v.tensor_tensor(out=b["t2"][:],
                                         in0=src_f32_ap[:, k * T:(k + 1) * T],
                                         in1=p[pbA][:], op=ALU.mult))
            mv = self.vc(self.v.tensor_tensor(out=b["t2"][:], in0=b["t2"][:],
                                              in1=p[pbB][:], op=ALU.subtract))
            if out_f32 is not None:
                mv = self.vc(self.v.tensor_copy(
                    out=out_f32[:, k * T:(k + 1) * T], in_=b["t2"][:]))
            if out_bf16 is not None:
                if gamma is not None:
                    mv = self.vc(self.v.tensor_scalar(
                        out=out_bf16[:, k * T:(k + 1) * T], in0=b["t2"][:],
                        scalar1=gamma[k], scalar2=beta[k],
                        op0=ALU.mult, op1=ALU.add))
                else:
                    mv = self.vc(self.v.tensor_copy(
                        out=out_bf16[:, k * T:(k + 1) * T], in_=b["t2"][:]))
        return mv

    def mamba_layer(self, dl):
        b, i, s = self.b, self.i, self.s
        fw = dl < self.NL
        li = dl % self.NL
        names = ["xh"] + ["rsA", "rsB"] * 4
        src = names[li]
        dst = ("fs" if fw else "bs") if li == self.NL - 1 else names[li + 1]

        m_alog = self.qi(self.q.dma_start(out=b["A"][:], in_=i["alog"][dl]))
        self.qi(self.q.dma_start(out=b["cws"][:], in_=i["cw"][dl]))
        self.qi(self.q.dma_start(out=b["pq"][:, 0:NDB], in_=i["dtb"][dl]))
        self.qi(self.q.dma_start(out=b["pq"][:, NDB:2 * NDB], in_=i["cb"][dl]))
        self.qi(self.q.dma_start(out=b["pq"][:, 2 * NDB:3 * NDB],
                                 in_=i["Dp"][dl]))
        self.qi(self.q.dma_start(out=b["pp"][:, 0:8], in_=i["nw"][dl]))
        self.qi(self.q.dma_start(out=b["pp"][:, 8:16], in_=i["nb"][dl]))
        m_dtw = self.qi(self.q.dma_start(out=b["rc"][0:64, 2048:2048 + DIN],
                                         in_=i["dtw"][dl]))
        m_xpa = self.qi(self.q.dma_start(
            out=b["rc"][:, 0:16 * 80], in_=i["xpwa"][dl]))
        m_xpb = self.qi(self.q.dma_start(
            out=b["rc"][:, 1280:1280 + 256], in_=i["xpwb"][dl]))
        self.wait("scalar", "LD", m_alog)
        ma = self.ac(self.a.activation(out=b["A"][:], in_=b["A"][:],
                                       func=AF.Exp))
        self.wait("vector", "AD", ma)
        self.vc(self.v.tensor_scalar_mul(out=b["A"][:], in0=b["A"][:],
                                         scalar1=-1.0))
        self.wait("vector", "LD", m_dtw)
        self.vi(self.v.tensor_copy(out=b["dtws"][:], in_=b["rc"][0:64, 2048:2048 + DIN]))
        self.wait("vector", "LD", m_xpa)
        self.vi(self.v.tensor_copy(out=b["xpa"][:], in_=b["rc"][:, 0:1280]))
        self.wait("vector", "LD", m_xpb)
        self.vi(self.v.tensor_copy(out=b["xpb"][:],
                                   in_=b["rc"][:, 1280:1280 + 256]))
        self.pi(self.g.memset(b["st"][:], 0.0))
        self.pi(self.g.memset(b["ut"][:], 0.0))
        self.pi(self.g.memset(b["dA0"][:], 0.0))
        self.pi(self.g.memset(b["dA1"][:], 0.0))
        self.barrier()

        chunks = range(self.NCH) if fw else range(self.NCH - 1, -1, -1)
        for c in chunks:
            self.mamba_chunk(dl, c, fw, src, dst)
            self.barrier()

    def u_ap(self, db, fw):
        off = 3 if fw else 0
        return self.b["u0"][:, db * 516 + off:db * 516 + off + self.T]

    def mamba_chunk(self, dl, c, fw, src, dst):
        b, i, s, p = self.b, self.i, self.s, self.p
        T = self.T
        cw0 = c * T
        m_rc = self.qi(self.q.dma_start(out=b["rc"][:], in_=s[src][c]))
        self.wait("scalar", "LD", m_rc)
        self.wait("vector", "LD", m_rc)
        self.wait("tensor", "LD", m_rc)
        gam = [b["pp"][:, k:k + 1] for k in range(8)]
        bet = [b["pp"][:, 8 + k:8 + k + 1] for k in range(8)]
        m_hn = self.layernorm(b["rc"][:], gamma=gam, beta=bet,
                              out_bf16=b["hn"][:, 0:NDO * T], nsec=NDO)

        u0_off = 3 if fw else 0
        mk_xz = {}
        m_evac = []
        for pr in range(NDB // 2):
            ring = b["wru"][:, (pr % 2) * 2048:(pr % 2) * 2048 + 2048]
            if pr >= 2:
                self.wait("sync", "TD", mk_xz[2 * pr - 3])
            m_w = self.qi(self.q.dma_start(
                out=ring, in_=s["wc_inwu"][dl][:, pr * 2048:(pr + 1) * 2048]))
            self.wait("tensor", "LD", m_w)
            self.wait("tensor", "VD", m_hn)
            for sub in range(2):
                dio = 2 * pr + sub
                pb = p[f"pb{2 + dio % 2}"]
                if dio >= 2:
                    self.wait("tensor", "AD", m_evac[dio - 2])
                for kb in range(8):
                    mt = self.ti(self.t.matmul(
                        pb[:], ring[:, sub * 1024 + kb * 128:
                                    sub * 1024 + (kb + 1) * 128],
                        b["hn"][:, kb * T:(kb + 1) * T],
                        start=(kb == 0), stop=(kb == 7)))
                mk_xz[dio] = mt
                self.wait("scalar", "TD", mt)
                me = self.ai(self.a.copy(
                    out=b["u0"][:, dio * 516 + u0_off:dio * 516 + u0_off + T],
                    in_=pb[:]))
                m_evac.append(me)

        tv = []
        for dio in range(NDB):
            self.wait("vector", "AD", m_evac[dio])
            if fw:
                self.vc(self.v.tensor_copy(
                    out=b["u0"][:, dio * 516:dio * 516 + 3],
                    in_=b["ut"][:, dio * 4:dio * 4 + 3]))
                mv = self.vc(self.v.tensor_copy(
                    out=b["ut"][:, dio * 4:dio * 4 + 3],
                    in_=b["u0"][:, dio * 516 + T:dio * 516 + T + 3]))
            else:
                self.vc(self.v.tensor_copy(
                    out=b["u0"][:, dio * 516 + T:dio * 516 + T + 3],
                    in_=b["ut"][:, dio * 4:dio * 4 + 3]))
                mv = self.vc(self.v.tensor_copy(
                    out=b["ut"][:, dio * 4:dio * 4 + 3],
                    in_=b["u0"][:, dio * 516:dio * 516 + 3]))
            tv.append(mv)

        m_u = []
        for db in range(NDB):
            if db >= 1:
                self.wait("vector", "AD", m_u[db - 1])
            for k in range(4):
                mdg = self.vi(self.v.tensor_scalar_mul(
                    out=b["diag"][:, k * 128:(k + 1) * 128], in0=b["ident"][:],
                    scalar1=b["cws"][:, db * 4 + k:db * 4 + k + 1]))
            self.wait("tensor", "VD", tv[db])
            self.wait("tensor", "VD", mdg)
            pb = p[f"pb{4 + db % 2}"]
            if db >= 2:
                self.wait("tensor", "AD", m_u[db - 2])
            for k in range(4):
                off = k if fw else (3 - k)
                mt = self.ti(self.t.matmul(
                    pb[:], b["diag"][:, k * 128:(k + 1) * 128],
                    b["u0"][:, db * 516 + off:db * 516 + off + T],
                    start=(k == 0), stop=(k == 3)))
            self.wait("scalar", "TD", mt)
            me = self.ai(self.a.activation(
                out=self.u_ap(db, fw), in_=pb[:], func=AF.Silu,
                bias=b["pq"][:, NDB + db:NDB + db + 1]))
            m_u.append(me)

        self.wait("tensor", "AD", m_u[-1])
        for kb in range(NDB):
            self.ti(self.t.matmul(
                p["pb2"][0:80, :], b["xpa"][:, kb * 80:(kb + 1) * 80],
                self.u_ap(kb, fw), start=(kb == 0), stop=(kb == NDB - 1)))
        for kb in range(NDB):
            mtb = self.ti(self.t.matmul(
                p["pb3"][0:16, :], b["xpb"][:, kb * 16:(kb + 1) * 16],
                self.u_ap(kb, fw), start=(kb == 0), stop=(kb == NDB - 1)))
        self.wait("scalar", "TD", mtb)
        self.ai(self.a.copy(out=b["dtr"][:], in_=p["pb2"][0:64, :]))
        self.ai(self.a.copy(out=b["brow"][:], in_=p["pb2"][64:80, :]))
        m_bc = self.ac(self.a.copy(out=b["crow"][:], in_=p["pb3"][0:16, :]))
        self.wait("gpsimd", "AD", m_bc)
        fl_b = s["bflat"][0:1, 2:2 + S * PITCH].rearrange(
            "o (s j) -> o s j", j=PITCH)[:, :, 0:T]
        fl_c = s["cflat"][0:1, 2:2 + S * PITCH].rearrange(
            "o (s j) -> o s j", j=PITCH)[:, :, 0:T]
        self.gi(self.g.dma_start(out=fl_b.squeeze(0), in_=b["brow"][:]))
        m_fl = self.gi(self.g.dma_start(out=fl_c.squeeze(0), in_=b["crow"][:]))
        self.wait("gpsimd", "GD", m_fl)
        self.gi(self.g.dma_start(
            out=b["bsl"][:], in_=s["bflat"].broadcast_to([128, S * PITCH + 2])))
        m_sl = self.gi(self.g.dma_start(
            out=b["csl"][:], in_=s["cflat"].broadcast_to([128, S * PITCH + 2])))

        mk_z = {}
        m_z = []
        for pr in range(NDB // 2):
            ring = b["wrz"][:, (pr % 2) * 2048:(pr % 2) * 2048 + 2048]
            if pr >= 2:
                self.wait("sync", "TD", mk_z[2 * pr - 3])
            m_w = self.qi(self.q.dma_start(
                out=ring, in_=s["wc_inwz"][dl][:, pr * 2048:(pr + 1) * 2048]))
            self.wait("tensor", "LD", m_w)
            for sub in range(2):
                db = 2 * pr + sub
                pb = p[f"pb{db % 2}"]
                if db >= 2:
                    self.wait("tensor", "AD", m_z[db - 2])
                for kb in range(8):
                    mt = self.ti(self.t.matmul(
                        pb[:], ring[:, sub * 1024 + kb * 128:
                                    sub * 1024 + (kb + 1) * 128],
                        b["hn"][:, kb * T:(kb + 1) * T],
                        start=(kb == 0), stop=(kb == 7)))
                mk_z[db] = mt
                self.wait("scalar", "TD", mt)
                me = self.ai(self.a.activation(
                    out=b["zy"][:, db * T:(db + 1) * T],
                    in_=pb[:], func=AF.Silu))
                m_z.append(me)

        mk = {"sp": {}, "dA": {}, "uDp": {}, "gate": {}, "dtmm": {}}
        for db in range(NDB):
            self.dblk(c, fw, db, mk, m_sl, m_bc)

        mk_ow = {}
        mres = []
        for pr in range(NDO // 2):
            ring = b["wro"][:, 0:4096]
            if pr >= 1:
                self.wait("sync", "TD", mk_ow[2 * pr - 1])
            m_w = self.qi(self.q.dma_start(
                out=ring, in_=s["wc_outw"][dl][:, pr * 4096:(pr + 1) * 4096]))
            self.wait("tensor", "LD", m_w)
            self.wait("tensor", "VD", mk["gate"][NDB - 1])
            for sub in range(2):
                do = 2 * pr + sub
                pb = p[f"pb{do % 2}"]
                if do >= 2:
                    self.wait("tensor", "VD", mres[do - 2])
                for kb in range(NDB):
                    mt = self.ti(self.t.matmul(
                        pb[:], ring[:, sub * 2048 + kb * 128:
                                    sub * 2048 + (kb + 1) * 128],
                        b["zy"][:, kb * T:(kb + 1) * T],
                        start=(kb == 0), stop=(kb == NDB - 1)))
                mk_ow[do] = mt
                self.wait("vector", "TD", mt)
                mv = self.vi(self.v.tensor_tensor(
                    out=b["rc"][:, do * T:(do + 1) * T],
                    in0=b["rc"][:, do * T:(do + 1) * T], in1=pb[:],
                    op=ALU.add))
                mres.append(mv)
        self.wait("gpsimd", "VD", mres[-1])
        self.gi(self.g.dma_start(out=s[dst][c], in_=b["rc"][:]))

    def dblk(self, c, fw, db, mk, m_sl, m_bc):
        b, p = self.b, self.p
        T = self.T
        par = db % 2
        dA = b["dA0"] if par == 0 else b["dA1"]
        pb_dt = p[f"pb{2 + par}"]
        dt = b["dt"][:, par * T:(par + 1) * T]

        if db < 2:
            self.wait("tensor", "AD", m_bc)
        else:
            self.wait("tensor", "AD", mk["sp"][db - 2])
        mt = self.ti(self.t.matmul(pb_dt[:],
                                   b["dtws"][:, db * 128:(db + 1) * 128],
                                   b["dtr"][:], start=True, stop=True))
        mk["dtmm"][db] = mt

        self.wait("scalar", "TD", mt)
        if db >= 2:
            self.wait("scalar", "VD", mk["uDp"][db - 2])
        self.ac(self.a.activation(out=dt, in_=pb_dt[:], func=AF.Exp,
                                  bias=b["pq"][:, db:db + 1]))
        msp = self.ac(self.a.activation(out=dt, in_=dt, func=AF.Ln,
                                        bias=b["kons"][:, 1:2]))
        mk["sp"][db] = msp
        for sct in range(S):
            o_ap = dA[:, sct * PITCH + 2:sct * PITCH + 2 + T]
            if not fw:
                o_ap = o_ap[:, ::-1]
            mda = self.ai(self.a.activation(
                out=o_ap, in_=dt, func=AF.Exp,
                scale=b["A"][:, db * S + sct:db * S + sct + 1]))
        mk["dA"][db] = mda

        self.wait("vector", "AD", mk["sp"][db])
        self.vc(self.v.tensor_tensor(out=b["dtu"][:, 2:2 + T], in0=dt,
                                     in1=self.u_ap(db, fw), op=ALU.mult))
        self.wait("vector", "GD", m_sl)
        if fw:
            in0 = b["dtu"][:, 0:PITCH].unsqueeze(1).broadcast_to(
                [128, S, PITCH])
            in1 = b["bsl"][:, 0:S * PITCH].rearrange("p (s j) -> p s j",
                                                     j=PITCH)
        else:
            in0 = b["dtu"][:, 2:516][:, ::-1].unsqueeze(1).broadcast_to(
                [128, S, PITCH])
            in1 = b["bsl"][:, 2:2 + S * PITCH].rearrange(
                "p (s j) -> p s j", j=PITCH)[:, :, ::-1]
        self.vc(self.v.tensor_tensor(
            out=b["qq"][:].rearrange("p (s j) -> p s j", j=PITCH),
            in0=in0, in1=in1, op=ALU.mult))
        self.vc(self.v.tensor_copy(
            out=b["qq"][:].rearrange("p (s j) -> p s j", j=PITCH)[:, :, 1],
            in_=b["st"][:, db * S:(db + 1) * S]))
        self.wait("vector", "AD", mk["dA"][db])
        self.vc(self.v.tensor_tensor_scan(
            out=b["h"][:], data0=dA[:], data1=b["qq"][:], initial=0.0,
            op0=ALU.mult, op1=ALU.add))
        self.vc(self.v.tensor_copy(
            out=b["st"][:, db * S:(db + 1) * S],
            in_=b["h"][:].rearrange("p (s j) -> p s j",
                                    j=PITCH)[:, :, PITCH - 1]))
        g3 = dA[:].rearrange("p (s j) -> p s j", j=PITCH)[:, :, 2:PITCH]
        h3 = b["h"][:].rearrange("p (s j) -> p s j", j=PITCH)[:, :, 2:PITCH]
        c3 = b["csl"][:, 0:S * PITCH].rearrange("p (s j) -> p s j",
                                                j=PITCH)[:, :, 2:PITCH]
        if not fw:
            c3 = c3[:, :, ::-1]
        self.vc(self.v.tensor_tensor(out=g3, in0=h3, in1=c3, op=ALU.mult))
        for w in (8, 4, 2, 1):
            self.vc(self.v.tensor_tensor(
                out=dA[:, 0:w * PITCH], in0=dA[:, 0:w * PITCH],
                in1=dA[:, w * PITCH:2 * w * PITCH], op=ALU.add))
        ytree = dA[:, 2:2 + T]
        if not fw:
            ytree = ytree[:, ::-1]
        muv = self.vc(self.v.scalar_tensor_tensor(
            out=b["dtu"][:, 2:2 + T], in0=self.u_ap(db, fw),
            scalar=b["pq"][:, 2 * S + db:2 * S + db + 1], in1=ytree,
            op0=ALU.mult, op1=ALU.add))
        mk["uDp"][db] = muv
        mg = self.vc(self.v.tensor_tensor(
            out=b["zy"][:, db * T:(db + 1) * T], in0=b["dtu"][:, 2:2 + T],
            in1=b["zy"][:, db * T:(db + 1) * T], op=ALU.mult))
        mk["gate"][db] = mg

    def ffn_phase(self, p3):
        b, i, s, p = self.b, self.i, self.s, self.p
        T = self.T
        for c in range(self.NCH):
            cw0 = c * T
            if not p3:
                m_rc = self.qi(self.q.dma_start(out=b["rc"][:], in_=i["xT"][c]))
                for e in ("vector", "scalar", "tensor"):
                    self.wait(e, "LD", m_rc)
            else:
                nfw = [b["pg"][:, k:k + 1] for k in range(NDO)]
                nfb_ = [b["pg"][:, NDO + k:NDO + k + 1] for k in range(NDO)]
                m_f = self.qi(self.q.dma_start(out=b["rc"][:], in_=s["fs"][c]))
                for e in ("vector", "scalar", "tensor"):
                    self.wait(e, "LD", m_f)
                self.layernorm(b["rc"][:], gamma=nfw, beta=nfb_,
                               out_bf16=b["hn"][:, 0:NDO * T], nsec=NDO)
                self.wait("sync", "VD")
                m_b = self.qi(self.q.dma_start(out=b["rc"][:], in_=s["bs"][c]))
                for e in ("vector", "scalar", "tensor"):
                    self.wait(e, "LD", m_b)
                m_hn = self.layernorm(b["rc"][:], gamma=nfw, beta=nfb_,
                                      out_bf16=b["zy"][:, 0:NDO * T],
                                      nsec=NDO)
                self.wait("sync", "VD", m_hn)
                pass
                m_a = self.qi(self.q.dma_start(out=b["rc"][:], in_=s["a"][c]))
                mres = []
                mlast = {}
                for do in range(NDO):
                    pb = p[f"pb{do % 2}"]
                    ring = b["wro"][:, (do % 2) * 2048:(do % 2) * 2048 + 2048]
                    if do >= 2:
                        self.wait("sync", "TD", mlast[do - 1])
                    m_w = self.qi(self.q.dma_start(
                        out=ring, in_=s["wc_mg"][:, do * 2048:(do + 1) * 2048]))
                    self.wait("tensor", "LD", m_w)
                    self.wait("tensor", "VD", m_hn)
                    if do >= 2:
                        self.wait("tensor", "VD", mres[do - 2])
                    for kb in range(16):
                        rhs = (b["hn"][:, kb * T:(kb + 1) * T] if kb < 8 else
                               b["zy"][:, (kb - 8) * T:(kb - 7) * T])
                        mt = self.ti(self.t.matmul(
                            pb[:], ring[:, kb * 128:(kb + 1) * 128], rhs,
                            start=(kb == 0), stop=(kb == 15)))
                    mlast[do] = mt
                    self.wait("vector", "LD", m_a)
                    self.wait("vector", "TD", mt)
                    self.vc(self.v.tensor_scalar_add(
                        out=b["t2"][:], in0=pb[:],
                        scalar1=b["pg"][:, 2 * NDO + do:2 * NDO + do + 1]))
                    mv = self.vc(self.v.tensor_tensor(
                        out=b["rc"][:, do * T:(do + 1) * T],
                        in0=b["rc"][:, do * T:(do + 1) * T], in1=b["t2"][:],
                        op=ALU.add))
                    mres.append(mv)
                for e in ("scalar", "tensor"):
                    self.wait(e, "VD", mres[-1])

            stage = self.cfg.get("stage")
            vv = b["dA0"][:].bitcast(F32)[:, 0:NDO * T]
            m_v = self.layernorm(b["rc"][:], out_f32=vv, nsec=NDO)
            if stage == "ln":
                self.wait("gpsimd", "VD", m_v)
                self.gi(self.g.dma_start(out=i["out"][c], in_=vv))
                self.barrier()
                continue
            gam = [b["pf"][:, NFB + NDO + k:NFB + NDO + k + 1]
                   for k in range(NDO)]
            bet = [b["pf"][:, NFB + 2 * NDO + k:NFB + 2 * NDO + k + 1]
                   for k in range(NDO)]
            self.wait("scalar", "VD", m_v)
            self.wait("tensor", "VD", m_v)
            m_w = self.layernorm(vv, gamma=gam, beta=bet,
                                 out_bf16=b["hn"][:, 0:NDO * T], nsec=NDO,
                                 pbA="pb2", pbB="pb3")
            if stage == "lng":
                self.wait("vector", "VD", m_w)
                mcp = self.vi(self.v.tensor_copy(out=vv, in_=b["hn"][:, 0:NDO * T]))
                self.wait("gpsimd", "VD", mcp)
                self.gi(self.g.dma_start(out=i["out"][c], in_=vv))
                self.barrier()
                continue
            mid = {}
            m_mid = []
            mk1 = {}
            for pr in range(NFB // 2):
                ring = b["wru"][:, (pr % 2) * 2048:(pr % 2) * 2048 + 2048]
                if pr >= 2:
                    self.wait("sync", "TD", mk1[2 * pr - 3])
                m_wt = self.qi(self.q.dma_start(
                    out=ring, in_=s["wc_w1"][:, pr * 2048:(pr + 1) * 2048]))
                self.wait("tensor", "LD", m_wt)
                self.wait("tensor", "VD", m_w)
                for sub in range(2):
                    f = 2 * pr + sub
                    pb = p[f"pb{4 + f % 2}"]
                    if f >= 2:
                        self.wait("tensor", "AD", m_mid[f - 2])
                    for kb in range(8):
                        mt = self.ti(self.t.matmul(
                            pb[:], ring[:, sub * 1024 + kb * 128:
                                        sub * 1024 + (kb + 1) * 128],
                            b["hn"][:, kb * T:(kb + 1) * T],
                            start=(kb == 0), stop=(kb == 7)))
                    mk1[f] = mt
                    self.wait("scalar", "TD", mt)
                    arena = b["dA1"] if f < 16 else b["qq"]
                    fo = (f % 16) * T
                    mid[f] = (arena, fo)
                    me = self.ai(self.a.activation(
                        out=arena[:, fo:fo + T], in_=pb[:], func=AF.Silu,
                        bias=b["pf"][:, f:f + 1]))
                    m_mid.append(me)
            if stage == "fc1":
                self.wait("vector", "AD", m_mid[-1])
                mcp = self.vi(self.v.tensor_copy(out=vv, in_=b["dA1"][:, 0:NDO * T]))
                self.wait("gpsimd", "VD", mcp)
                self.gi(self.g.dma_start(out=i["out"][c], in_=vv))
                self.barrier()
                continue
            if self.cfg.get("bar_fc"):
                self.barrier()
            mout = []
            for do in range(NDO):
                ring2 = b["wrz"][:, (do % 2) * 2048:(do % 2) * 2048 + 2048]
                pb = p[f"pb{do % 2}"]
                for half in range(2):
                    m_wt = self.qi(self.q.dma_start(
                        out=ring2,
                        in_=s["wc_w2"][:, do * 4096 + half * 2048:
                                       do * 4096 + (half + 1) * 2048]))
                    self.wait("tensor", "LD", m_wt)
                    if half == 0:
                        self.wait("tensor", "AD", m_mid[-1])
                        if do >= 2:
                            self.wait("tensor", "VD", mout[do - 2])
                    for kb in range(16):
                        ff = half * 16 + kb
                        ar, fo = mid[ff]
                        mt = self.ti(self.t.matmul(
                            pb[:], ring2[:, kb * 128:(kb + 1) * 128],
                            ar[:, fo:fo + T],
                            start=(half == 0 and kb == 0),
                            stop=(half == 1 and kb == 15)))
                    self.wait("sync", "TD", mt)
                self.wait("vector", "TD", mt)
                self.vc(self.v.tensor_copy(out=b["t2"][:, 0:256],
                                           in_=b["t2"][:, 0:256]))
                self.vc(self.v.tensor_scalar(
                    out=b["t2"][:], in0=pb[:],
                    scalar1=b["pf"][:, NFB + do:NFB + do + 1], scalar2=0.25,
                    op0=ALU.add, op1=ALU.mult))
                self.vc(self.v.tensor_tensor(
                    out=b["rc"][:, do * T:(do + 1) * T],
                    in0=b["rc"][:, do * T:(do + 1) * T], in1=b["t2"][:],
                    op=ALU.add))
                mv = self.vi(self.v.scalar_tensor_tensor(
                    out=b["rc"][:, do * T:(do + 1) * T],
                    in0=vv[:, do * T:(do + 1) * T], scalar=0.5,
                    in1=b["rc"][:, do * T:(do + 1) * T],
                    op0=ALU.mult, op1=ALU.add))
                mout.append(mv)
            if stage == "fc2":
                self.wait("gpsimd", "VD", mout[-1])
                self.gi(self.g.dma_start(out=i["out"][c], in_=b["rc"][:]))
                self.barrier()
                continue
            self.wait("gpsimd", "VD", mout[-1])
            dst = i["out"] if p3 else s["a"]
            self.gi(self.g.dma_start(out=dst[c], in_=b["rc"][:]))
            if not p3:
                for e in ("vector", "scalar", "tensor"):
                    self.wait(e, "VD", mout[-1])
                m_x = self.layernorm(b["rc"][:], out_f32=vv, nsec=NDO,
                                     pbA="pb2", pbB="pb3")
                self.wait("gpsimd", "VD", m_x)
                self.gi(self.g.dma_start(out=s["xh"][c], in_=vv))
            self.barrier()


def _prep_inputs(inputs, cfg):
    L = cfg["L"]
    T = cfg["T"]
    NL = cfg["NLAYER"]
    f = np.float32

    def pmajor(wT):
        K, N = wT.shape
        return np.ascontiguousarray(
            wT.reshape(K // 128, 128, N // 128, 128)
            .transpose(1, 2, 0, 3).reshape(128, -1))

    def colpack(vals):
        n = vals.shape[0]
        nb_ = n // 128
        return np.ascontiguousarray(
            vals.reshape(nb_, 128, -1).transpose(1, 0, 2).reshape(128, -1))

    d = {}
    x = np.asarray(inputs["L0_attn_x"], dtype=f)
    B = x.shape[0]

    def lay(name):
        fwv = np.asarray(inputs[f"L0_attn_fw_{name}"], dtype=f)
        bwv = np.asarray(inputs[f"L0_attn_bw_{name}"], dtype=f)
        return [fwv[l] for l in range(NL)] + [bwv[l] for l in range(NL)]

    inW = lay("in_W")
    d["inwu"] = np.stack([pmajor(np.ascontiguousarray(w[:DIN].T))
                          for w in inW]).astype(f)
    d["inwz"] = np.stack([pmajor(np.ascontiguousarray(w[DIN:].T))
                          for w in inW]).astype(f)
    d["outw"] = np.stack([pmajor(np.ascontiguousarray(w.T))
                          for w in lay("out_W")]).astype(f)
    d["xpwa"] = np.stack([np.ascontiguousarray(
        w.T[:, 0:80].reshape(16, 128, 80).transpose(1, 0, 2).reshape(128, -1))
        for w in lay("xproj_W")]).astype(f)
    d["xpwb"] = np.stack([np.ascontiguousarray(
        w.T[:, 80:96].reshape(16, 128, 16).transpose(1, 0, 2).reshape(128, -1))
        for w in lay("xproj_W")]).astype(f)
    d["dtw"] = np.stack([np.ascontiguousarray(w.T)
                         for w in lay("dt_W")]).astype(f)
    d["alog"] = np.stack([colpack(v) for v in lay("Alog")]).astype(f)
    d["cw"] = np.stack([colpack(v) for v in lay("conv_w")]).astype(f)
    d["cb"] = np.stack([colpack(v[:, None]) for v in lay("conv_b")]).astype(f)
    d["dtb"] = np.stack([colpack(v[:, None]) for v in lay("dt_b")]).astype(f)
    d["Dp"] = np.stack([colpack(v[:, None]) for v in lay("D")]).astype(f)
    d["nw"] = np.stack([colpack(v[:, None]) for v in lay("norm_w")]).astype(f)
    d["nb"] = np.stack([colpack(v[:, None]) for v in lay("norm_b")]).astype(f)
    d["fc1q"] = pmajor(np.asarray(inputs["L0_ffn_fc1_q"])).astype(np.int32)
    d["fc2q"] = pmajor(np.asarray(inputs["L0_ffn_fc2_q"])).astype(np.int32)
    s1 = np.asarray(inputs["L0_ffn_fc1_scales"], dtype=f)
    d["fc1s"] = np.ascontiguousarray(np.broadcast_to(
        s1.reshape(8, NFB, 128).transpose(1, 0, 2)[None],
        (128, NFB, 8, 128)).reshape(128, -1))
    s2 = np.asarray(inputs["L0_ffn_fc2_scales"], dtype=f)
    d["fc2s"] = np.ascontiguousarray(np.broadcast_to(
        s2.reshape(NFB, NDO, 128).transpose(1, 0, 2)[None],
        (128, NDO, NFB, 128)).reshape(128, -1))
    d["b1"] = colpack(np.asarray(inputs["L0_ffn_fc1_bias"], f)[:, None])
    d["b2"] = colpack(np.asarray(inputs["L0_ffn_fc2_bias"], f)[:, None])
    d["lng"] = colpack(np.asarray(inputs["L0_ffn_ln_gamma"], f)[:, None])
    d["lnb"] = colpack(np.asarray(inputs["L0_ffn_ln_beta"], f)[:, None])
    d["nfw"] = colpack(np.asarray(inputs["L0_attn_normf_w"], f)[:, None])
    d["nfb"] = colpack(np.asarray(inputs["L0_attn_normf_b"], f)[:, None])
    d["mgw"] = pmajor(np.ascontiguousarray(
        np.asarray(inputs["L0_attn_merge_W"], f).T)).astype(f)
    d["mgb"] = colpack(np.asarray(inputs["L0_attn_merge_b"], f)[:, None])
    ow = np.zeros((128, 2), f)
    ow[:, 0] = 1.0 / 1024.0
    ow[:, 1] = 1.0
    d["onesw"] = ow
    d["onescol"] = np.ones((1, 128), f)
    kons = np.zeros((128, 2), f)
    kons[:, 0] = 1e-5
    kons[:, 1] = 1.0
    d["kons"] = kons
    d["ident"] = np.eye(128).astype(ml_dtypes.bfloat16)

    in_maps = []
    for bb in range(B):
        m = dict(d)
        xt = x[bb, :L].reshape(L // T, T, NDO, 128)
        m["xT"] = np.ascontiguousarray(
            xt.transpose(0, 3, 2, 1)).reshape(L // T, 128, NDO * T).astype(f)
        in_maps.append(m)
    return in_maps


def _run_raw(inputs, cfg):
    in_maps = _prep_inputs(inputs, cfg)
    nc = bass.Bass("TRN2", target_bir_lowering=False, debug=False)
    mk = MK(nc, cfg)
    mk.build()
    return run_bass_kernel_spmd(nc, in_maps, core_ids=list(range(len(in_maps))),
                                trace_cores=[0])


def kernel(**inputs):
    cfg = dict(CFG)
    res = _run_raw(inputs, cfg)
    outs = []
    T = cfg["T"]
    for r in res.results:
        o = np.asarray(r["out"]).reshape(cfg["L"] // T, 128, NDO, T)
        outs.append(np.ascontiguousarray(
            o.transpose(0, 3, 2, 1).reshape(cfg["L"], D)))
    return np.stack(outs).astype(np.float32)
